# version 01
# baseline (speedup 1.0000x reference)
import numpy as np
from contextlib import ExitStack

import concourse.bass as bass
import concourse.mybir as mybir
from concourse.bass_utils import run_bass_kernel_spmd

F32 = mybir.dt.float32
BF16 = mybir.dt.bfloat16
AF = mybir.ActivationFunctionType
ALU = mybir.AluOpType
AX = mybir.AxisListType

D = 2048
TOK = 2048
TC = 512
NT = TOK // TC
ND = D // 128
SEQ = 4096
NCORES = 8

LN_EPS = 1e-5


class Sem:
    def __init__(self, nc, ctx, name):
        self.h = ctx.enter_context(nc.semaphore(name))
        self.v = 0

    def inc(self, n=1):
        self.v += n
        return self.v


class Prog:
    def __init__(self, num_devices=8, detect_races=True):
        self.nc = bass.Bass("TRN2", target_bir_lowering=False,
                            num_devices=num_devices,
                            detect_race_conditions=detect_races)
        self.ctx = ExitStack()
        self.ctx.__enter__()
        self._i = 0

    def sem(self, name=None):
        self._i += 1
        return Sem(self.nc, self.ctx, name or f"s{self._i}")

    def sbuf(self, shape, dtype, name=None):
        self._i += 1
        return self.ctx.enter_context(
            self.nc.sbuf_tensor(name or f"b{self._i}", shape, dtype))

    def psum(self, shape, dtype=F32, name=None):
        self._i += 1
        return self.ctx.enter_context(
            self.nc.psum_tensor(name or f"p{self._i}", shape, dtype))

    def dram(self, name, shape, dtype, kind="Internal"):
        return self.nc.dram_tensor(name, shape, dtype, kind=kind).ap()


class Arena:

    def __init__(self, p: Prog, name, words):
        self.h = p.sbuf([128, words], F32, name)
        self.hb = self.h.bitcast(BF16)
        self.words = words
        self.cur = 0

    def f32(self, nwords):
        off = self.cur
        self.cur += nwords
        assert self.cur <= self.words, f"arena overflow {self.cur}>{self.words}"
        return self.h[:, off:off + nwords]

    def bf16(self, nwords):
        off = self.cur
        self.cur += nwords
        assert self.cur <= self.words, f"arena overflow {self.cur}>{self.words}"
        return self.hb[:, 2 * off:2 * (off + nwords)]

    def reset(self, to=0):
        self.cur = to


class Common:

    READY = 3

    def __init__(self, p: Prog):
        self.p = p
        self.ones_col = p.sbuf([128, 1], F32, "ones_col")
        self.ones_row = p.sbuf([1, 128], F32, "ones_row")
        self.ones_col_b = p.sbuf([128, 1], BF16, "ones_col_b")
        self.ones_row_b = p.sbuf([1, 128], BF16, "ones_row_b")
        self.ident_b = p.sbuf([128, 128], BF16, "ident_b")
        self.ident_f = p.sbuf([128, 128], F32, "ident_f")
        self.ones_sq = p.sbuf([128, 128], F32, "ones_sq")
        self.cvals = [LN_EPS, 1e-6, float(np.finfo(np.float32).eps), 1.0, 0.5]
        self.c128 = p.sbuf([128, len(self.cvals)], F32, "c128")
        self.init = p.sem("init")

    def col(self, value):
        i = self.cvals.index(value)
        return self.c128[:, i:i + 1]

    def emit_vector(self, v):
        nc = self.p.nc
        nc.vector.memset(self.ones_col[:], 1.0)
        nc.vector.memset(self.ones_row[:], 1.0)
        nc.vector.memset(self.ones_col_b[:], 1.0)
        nc.vector.memset(self.ones_row_b[:], 1.0)
        nc.vector.memset(self.ones_sq[:], 1.0)
        for i, val in enumerate(self.cvals):
            nc.vector.memset(self.c128[:, i:i + 1], float(val))
        nc.vector.memset(self.ident_f[:], 0.0).then_inc(self.init.h, 1)

    def emit_gpsimd(self, g):
        nc = self.p.nc
        g.wait_ge(self.init.h, 1)
        nc.gpsimd.affine_select(
            self.ident_f[:], self.ones_sq[:], [[-1, 128]], ALU.is_equal, 0.0,
            base=0, channel_multiplier=1).then_inc(self.init.h, 1)
        g.wait_ge(self.init.h, 2)
        nc.gpsimd.tensor_copy(self.ident_b[:], self.ident_f[:]).then_inc(self.init.h, 1)



NROW = 10


class LNCore:

    def __init__(self, p: Prog, com: Common, tag: str, nd=ND, rms_eps=None,
                 tc=TC):
        self.p, self.com, self.nd, self.tc = p, com, nd, tc
        self.rms_eps = rms_eps
        self.nrow = 21 if rms_eps is not None else 10
        self.xsq = [p.sbuf([128, tc], F32, f"{tag}xsq{i}") for i in range(2)]
        self.rows = p.sbuf([1, tc * 8], F32, f"{tag}rows")
        self.s_sq = p.sem(f"{tag}sq")
        self.s_mm = p.sem(f"{tag}mm")
        self.s_row = p.sem(f"{tag}row")
        self.s_bc = p.sem(f"{tag}bc")

    def r(self, i):
        return self.rows[0:1, i * self.tc:(i + 1) * self.tc]

    def scalar_chunk(self, s, c, x_sb, xcol):
        nc = self.p.nc
        nd = self.nd
        for k in range(nd):
            if c * nd + k >= 2:
                s.wait_ge(self.s_mm.h, c * nd + k - 1)
            nc.scalar.activation(self.xsq[k % 2][:], x_sb[:, xcol(k)],
                                 AF.Square).then_inc(self.s_sq.h, 1)
        s.wait_ge(self.s_row.h, self.nrow * c + 4)
        nc.scalar.activation(self.r(4), self.r(3), AF.Sqrt).then_inc(self.s_row.h, 1)
        if self.rms_eps is not None:
            s.wait_ge(self.s_row.h, self.nrow * c + 14)
            nc.scalar.activation(self.r(4), self.r(1), AF.Sqrt).then_inc(self.s_row.h, 1)

    def tensor_chunk(self, t, c, x_sb, xcol, ps_a, ps_b):
        nc, com = self.p.nc, self.com
        nd = self.nd
        tc = self.tc
        for k in range(nd):
            nc.tensor.matmul(ps_a[0:1, 0:tc], com.ones_col[:], x_sb[:, xcol(k)],
                             start=(k == 0), stop=(k == nd - 1))
        for k in range(nd):
            t.wait_ge(self.s_sq.h, c * nd + k + 1)
            nc.tensor.matmul(ps_b[0:1, 0:tc], com.ones_col[:], self.xsq[k % 2][:],
                             start=(k == 0), stop=(k == nd - 1)).then_inc(self.s_mm.h, 1)
        t.wait_ge(self.s_row.h, self.nrow * (c + 1))
        nc.tensor.matmul(ps_a[:, 0:tc], com.ones_row[:], self.r(0), start=True, stop=True)
        nc.tensor.matmul(ps_b[:, 0:tc], com.ones_row[:], self.r(7), start=True, stop=True
                         ).then_inc(self.s_bc.h, 1)

    def vector_chunk(self, v, c, ps_a, ps_b, eps=LN_EPS, inv_n=None):
        nc = self.p.nc
        r0 = self.nrow * c
        inv_n = inv_n if inv_n is not None else 1.0 / (self.nd * 128)
        S = self.s_row
        W = lambda k: v.wait_ge(S.h, r0 + k)
        v.wait_ge(self.s_mm.h, self.nd * (c + 1))
        nc.vector.tensor_scalar_mul(self.r(0), ps_a[0:1, 0:self.tc], inv_n).then_inc(S.h, 1)
        W(1)
        nc.vector.tensor_tensor(self.r(1), self.r(0), self.r(0), ALU.mult).then_inc(S.h, 1)
        W(2)
        nc.vector.scalar_tensor_tensor(self.r(3), ps_b[0:1, 0:self.tc], inv_n, self.r(1),
                                       ALU.mult, ALU.subtract).then_inc(S.h, 1)
        W(3)
        nc.vector.tensor_scalar_add(self.r(3), self.r(3), float(eps)).then_inc(S.h, 1)
        W(5)
        nc.vector.reciprocal(self.r(5), self.r(4)).then_inc(S.h, 1)
        W(6)
        nc.vector.tensor_tensor(self.r(6), self.r(5), self.r(5), ALU.mult).then_inc(S.h, 1)
        W(7)
        nc.vector.tensor_tensor(self.r(6), self.r(6), self.r(3), ALU.mult).then_inc(S.h, 1)
        W(8)
        nc.vector.tensor_scalar(self.r(6), self.r(6), -0.5, 1.5, ALU.mult, ALU.add).then_inc(S.h, 1)
        W(9)
        nc.vector.tensor_tensor(self.r(7), self.r(5), self.r(6), ALU.mult).then_inc(S.h, 1)
        if self.rms_eps is None:
            return
        W(10)
        nc.vector.tensor_tensor(self.r(6), self.r(7), self.r(7), ALU.mult).then_inc(S.h, 1)
        W(11)
        nc.vector.tensor_tensor(self.r(1), self.r(6), self.r(3), ALU.mult).then_inc(S.h, 1)
        W(12)
        nc.vector.scalar_tensor_tensor(self.r(1), self.r(6), -float(eps), self.r(1),
                                       ALU.mult, ALU.add).then_inc(S.h, 1)
        W(13)
        nc.vector.tensor_scalar_add(self.r(1), self.r(1), float(self.rms_eps)).then_inc(S.h, 1)
        W(15)
        nc.vector.reciprocal(self.r(5), self.r(4)).then_inc(S.h, 1)
        W(16)
        nc.vector.tensor_tensor(self.r(6), self.r(5), self.r(5), ALU.mult).then_inc(S.h, 1)
        W(17)
        nc.vector.tensor_tensor(self.r(6), self.r(6), self.r(1), ALU.mult).then_inc(S.h, 1)
        W(18)
        nc.vector.tensor_scalar(self.r(6), self.r(6), -0.5, 1.5, ALU.mult, ALU.add).then_inc(S.h, 1)
        W(19)
        nc.vector.tensor_tensor(self.r(5), self.r(5), self.r(6), ALU.mult).then_inc(S.h, 1)
        W(20)
        nc.vector.tensor_tensor(self.r(7), self.r(7), self.r(5), ALU.mult).then_inc(S.h, 1)



def emit_normalize(nc, v, snorm, snorm2, base, x_sb, xcol, xn_sb, xncol,
                   ps_a, ps_b, nd=ND, sub3=None):
    mm = None
    for k in range(nd + 2):
        if k < nd:
            if k >= 3:
                v.wait_ge(snorm2.h, base + k - 2)
            nc.vector.tensor_tensor(sub3[:, (k % 3) * TC:(k % 3 + 1) * TC],
                                    x_sb[:, xcol(k)], ps_a[:], ALU.subtract
                                    ).then_inc(snorm.h, 1)
        if k >= 2:
            j = k - 2
            v.wait_ge(snorm.h, base + j + 1)
            mm = nc.vector.tensor_tensor(xn_sb[:, xncol(j)],
                                         sub3[:, (j % 3) * TC:(j % 3 + 1) * TC],
                                         ps_b[:], ALU.mult).then_inc(snorm2.h, 1)
    return mm



NH_FFN = 64


def emit_ffn(p: Prog, com: Common, tag: str, x_in, x_out, wgu, wd, prev=None):
    nc = p.nc
    ln = LNCore(p, com, tag + "L")
    x_sb = p.sbuf([128, ND * TC], F32, tag + "x")
    xn_sb = p.sbuf([128, ND * TC], BF16, tag + "xn")
    sub3 = p.sbuf([128, 3 * TC], F32, tag + "sub")
    h_sb = p.sbuf([128, NH_FFN * TC], BF16, tag + "h")
    gs = [p.sbuf([128, TC], F32, f"{tag}gs{i}") for i in range(2)]
    wgu_b = [p.sbuf([128, 2 * 2048], BF16, f"{tag}wg{i}") for i in range(2)]
    wd_b = [p.sbuf([128, 8192], BF16, f"{tag}wd{i}") for i in range(2)]
    ob = [p.sbuf([128, TC], F32, f"{tag}ob{i}") for i in range(2)]
    PS = [p.psum([128, TC], F32, f"{tag}ps{i}") for i in range(8)]

    sx = p.sem(tag + "sx")
    sxn = p.sem(tag + "sxn")
    swg = p.sem(tag + "swg")
    smg = p.sem(tag + "smg")
    ssil = p.sem(tag + "ssil")
    sh = p.sem(tag + "sh")
    swd = p.sem(tag + "swd")
    smd = p.sem(tag + "smd")
    sout = p.sem(tag + "sout")
    sst = p.sem(tag + "sst")
    snrm = p.sem(tag + "snrm")
    snrm2 = p.sem(tag + "snrm2")

    xcol = lambda k: slice(k * TC, (k + 1) * TC)
    hcol = lambda m: slice(m * TC, (m + 1) * TC)

    def f_sync(sync):
        if prev is not None:
            sync.wait_ge(prev[0].h, prev[1])
        for c in range(NT):
            if c >= 1:
                sync.wait_ge(sout.h, c * ND)
            for k in range(ND):
                sync.dma_start(out=x_sb[:, xcol(k)],
                               in_=x_in[k * 128:(k + 1) * 128, c * TC:(c + 1) * TC]
                               ).then_inc(sx.h, 16)
            for j in range(ND):
                g = c * ND + j
                sync.wait_ge(sout.h, g + 1)
                sync.dma_start(out=x_out[j * 128:(j + 1) * 128, c * TC:(c + 1) * TC],
                               in_=ob[j % 2][:]).then_inc(sst.h, 16)
        sync.wait_ge(sst.h, 16 * ND * NT)
        sst.v = 16 * ND * NT
        sx.v = 16 * ND * NT

    def f_gpsimd(g):
        for c in range(NT):
            for m in range(NH_FFN):
                gg = c * NH_FFN + m
                if gg >= 2:
                    g.wait_ge(smg.h, gg - 1)
                g.dma_start(out=wgu_b[gg % 2][:, 0:2048], in_=wgu[m]
                            ).then_inc(swg.h, 16)
                g.dma_start(out=wgu_b[gg % 2][:, 2048:4096], in_=wgu[64 + m]
                            ).then_inc(swg.h, 16)
            for dout in range(ND):
                gd = c * ND + dout
                if gd >= 2:
                    g.wait_ge(smd.h, gd - 1)
                for q in range(4):
                    g.dma_start(out=wd_b[gd % 2][:, q * 2048:(q + 1) * 2048],
                                in_=wd[dout][:, q * 2048:(q + 1) * 2048]
                                ).then_inc(swd.h, 16)

    def f_tensor(t):
        t.wait_ge(com.init.h, Common.READY)
        for c in range(NT):
            t.wait_ge(sx.h, 256 * (c + 1))
            ln.tensor_chunk(t, c, x_sb, xcol, PS[0], PS[1])
            t.wait_ge(snrm2.h, ND * (c + 1))
            for m in range(NH_FFN):
                gg = c * NH_FFN + m
                t.wait_ge(swg.h, 32 * (gg + 1))
                if gg >= 2:
                    t.wait_ge(ssil.h, gg - 1)
                    t.wait_ge(sh.h, gg - 1)
                pg, pu = PS[2 + gg % 2], PS[4 + gg % 2]
                for k in range(ND):
                    nc.tensor.matmul(pg[:], wgu_b[gg % 2][:, k * 128:(k + 1) * 128],
                                     xn_sb[:, xcol(k)],
                                     start=(k == 0), stop=(k == ND - 1))
                for k in range(ND):
                    mm = nc.tensor.matmul(pu[:],
                                          wgu_b[gg % 2][:, 2048 + k * 128:2048 + (k + 1) * 128],
                                          xn_sb[:, xcol(k)],
                                          start=(k == 0), stop=(k == ND - 1))
                mm.then_inc(smg.h, 1)
            t.wait_ge(sh.h, (c + 1) * NH_FFN)
            for dout in range(ND):
                gd = c * ND + dout
                t.wait_ge(swd.h, 64 * (gd + 1))
                if gd >= 2:
                    t.wait_ge(sout.h, gd - 1)
                pd = PS[6 + gd % 2]
                for k in range(NH_FFN):
                    mm = nc.tensor.matmul(pd[:], wd_b[gd % 2][:, k * 128:(k + 1) * 128],
                                          h_sb[:, hcol(k)],
                                          start=(k == 0), stop=(k == NH_FFN - 1))
                mm.then_inc(smd.h, 1)

    def f_scalar(s):
        s.wait_ge(com.init.h, Common.READY)
        for c in range(NT):
            s.wait_ge(sx.h, 256 * (c + 1))
            ln.scalar_chunk(s, c, x_sb, xcol)
            for m in range(NH_FFN):
                gg = c * NH_FFN + m
                s.wait_ge(smg.h, gg + 1)
                if gg >= 2:
                    s.wait_ge(sh.h, gg - 1)
                nc.scalar.activation(gs[gg % 2][:], PS[2 + gg % 2][:],
                                     AF.Silu).then_inc(ssil.h, 1)

    def f_vector(v):
        v.wait_ge(com.init.h, Common.READY)
        for c in range(NT):
            ln.vector_chunk(v, c, PS[0], PS[1])
            v.wait_ge(ln.s_bc.h, c + 1)
            if c >= 1:
                v.wait_ge(smg.h, c * NH_FFN)
            emit_normalize(nc, v, snrm, snrm2, c * ND, x_sb, xcol, xn_sb,
                           xcol, PS[0], PS[1], sub3=sub3)
            for m in range(NH_FFN):
                gg = c * NH_FFN + m
                v.wait_ge(ssil.h, gg + 1)
                nc.vector.tensor_tensor(h_sb[:, hcol(m)], gs[gg % 2][:],
                                        PS[4 + gg % 2][:], ALU.mult).then_inc(sh.h, 1)
            for dout in range(ND):
                gd = c * ND + dout
                v.wait_ge(smd.h, gd + 1)
                if gd >= 2:
                    v.wait_ge(sst.h, 16 * (gd - 1))
                nc.vector.tensor_add(ob[gd % 2][:], x_sb[:, xcol(dout)],
                                     PS[6 + gd % 2][:]).then_inc(sout.h, 1)

    return dict(sync=f_sync, gpsimd=f_gpsimd, tensor=f_tensor,
                scalar=f_scalar, vector=f_vector,
                done=(sst, 16 * ND * NT))



ATT_EPS = 1e-9
ATT_SCALE = float(128.0 ** -0.5)


def emit_attn_a(p: Prog, com: Common, tag: str, x_in, x_out, wqkv, wout,
                carrysel, prev=None):
    nc = p.nc
    xn_d = [p.dram(f"{tag}xnd{c}", [128, ND * TC], BF16) for c in range(NT)]
    qp_d = p.dram(tag + "qpd", [ND, 128, TOK], BF16)
    kvc_d = p.dram(tag + "kvcd", [ND, 128, TOK], BF16)
    kc_d = p.dram(tag + "kcd", [ND, 128, TOK], BF16)
    cc_in = p.dram(tag + "ccin", [128, 32], F32)
    cc_out = p.dram(tag + "ccout", [256, 32], F32)

    ln = LNCore(p, com, tag + "L")
    ones_tc = p.sbuf([128, TOK], F32, tag + "ones")
    x_sb = p.sbuf([128, ND * TC], F32, tag + "x")
    A = Arena(p, tag + "ar", 29184)
    xn_sb = A.bf16(4096)
    sub3 = A.f32(3 * TC)
    xnc = [A.bf16(4096) for i in range(2)]
    wbk = [A.bf16(1024) for i in range(2)]
    wbv = [A.bf16(1024) for i in range(2)]
    wbq = [A.bf16(1024) for i in range(2)]
    phi_t = [A.f32(TC) for i in range(4)]
    kp_sb = A.f32(TOK)
    kv_sb = A.f32(TOK)
    qp_par = [A.bf16(1024)]
    kvc_par = [A.bf16(1024)]
    kc_par = [A.bf16(1024)]
    A.reset(0)
    rl = [A.f32(3 * TC) for i in range(2)]
    sc4 = A.f32(4 * TC)
    att_sb = A.bf16(4096)
    wo_b = [A.bf16(1024) for i in range(2)]
    ob = [A.f32(TC) for i in range(2)]
    totals = p.sbuf([128, 32], F32, tag + "tot")
    carr = [p.sbuf([128, 32], F32, f"{tag}ca{i}") for i in range(2)]
    carry = p.sbuf([128, 32], F32, tag + "carry")
    ct2 = p.sbuf([128, 32], F32, tag + "ct2")
    csel = p.sbuf([128, 2], F32, tag + "csel")
    PS = [p.psum([128, TC], F32, f"{tag}ps{i}") for i in range(8)]

    sx = p.sem(tag + "sx")
    sxn = p.sem(tag + "sxn")
    sxnst = p.sem(tag + "sxnst")
    sxnl = p.sem(tag + "sxnl")
    swq = p.sem(tag + "swq")
    sma = p.sem(tag + "sma")
    sp1 = p.sem(tag + "sp1")
    sp2 = p.sem(tag + "sp2")
    sfree = p.sem(tag + "sfree")
    ssc = p.sem(tag + "ssc")
    ssta = p.sem(tag + "ssta")
    stot = p.sem(tag + "stot")
    scsl = p.sem(tag + "scsl")
    scc = p.sem(tag + "scc")
    scr = p.sem(tag + "scr")
    scar = p.sem(tag + "scar")
    sbl = p.sem(tag + "sbl")
    sbs = p.sem(tag + "sbs")
    sbv = p.sem(tag + "sbv")
    sxb = p.sem(tag + "sxb")
    swo = p.sem(tag + "swo")
    smo = p.sem(tag + "smo")
    sob = p.sem(tag + "sob")
    sst = p.sem(tag + "sst")
    snrm = p.sem(tag + "snrm")
    snrm2 = p.sem(tag + "snrm2")
    ssc2 = p.sem(tag + "ssc2")

    xcol = lambda k: slice(k * TC, (k + 1) * TC)
    tc_ = lambda c: slice(c * TC, (c + 1) * TC)

    def f_sync(sync):
        if prev is not None:
            sync.wait_ge(prev[0].h, prev[1])
        sync.dma_start(out=csel[:], in_=carrysel).then_inc(scsl.h, 16)
        for c in range(NT):
            for k in range(ND):
                sync.dma_start(out=x_sb[:, xcol(k)],
                               in_=x_in[k * 128:(k + 1) * 128, tc_(c)]
                               ).then_inc(sx.h, 16)
            sync.wait_ge(snrm2.h, ND * (c + 1))
            for k in range(ND):
                sync.dma_start(out=xn_d[c][:, xcol(k)], in_=xn_sb[:, xcol(k)]
                               ).then_inc(sxnst.h, 16)
        for f in range(ND):
            sync.wait_ge(ssc2.h, 4 * (f + 1))
            sync.dma_start(out=qp_d[f], in_=qp_par[0][:]).then_inc(ssta.h, 16)
            sync.dma_start(out=kvc_d[f], in_=kvc_par[0][:]).then_inc(ssta.h, 16)
            sync.dma_start(out=kc_d[f], in_=kc_par[0][:]).then_inc(ssta.h, 16)
        sync.wait_ge(ssc2.h, 4 * ND)
        sync.dma_start(out=cc_in, in_=totals[:]).then_inc(stot.h, 16)
        for c in range(NT):
            if c >= 1:
                sync.wait_ge(sob.h, c * ND)
            for k in range(ND):
                sync.dma_start(out=x_sb[:, xcol(k)],
                               in_=x_in[k * 128:(k + 1) * 128, tc_(c)]
                               ).then_inc(sxb.h, 16)
            for j in range(ND):
                g = c * ND + j
                sync.wait_ge(sob.h, g + 1)
                sync.dma_start(out=x_out[j * 128:(j + 1) * 128, tc_(c)],
                               in_=ob[j % 2][:]).then_inc(sst.h, 16)
        sync.wait_ge(sst.h, 16 * ND * NT)

    def f_gpsimd(g):
        for f in range(ND):
            if f >= 2:
                g.wait_ge(sma.h, 12 * (f - 1))
            g.dma_start(out=wbk[f % 2][:], in_=wqkv[16 + f]).then_inc(swq.h, 16)
            g.dma_start(out=wbv[f % 2][:], in_=wqkv[32 + f]).then_inc(swq.h, 16)
            g.dma_start(out=wbq[f % 2][:], in_=wqkv[f]).then_inc(swq.h, 16)
        g.wait_ge(ssta.h, 48 * ND)
        g.wait_ge(stot.h, 16)
        nc.gpsimd.collective_compute(
            "AllGather", ALU.bypass,
            replica_groups=[[0, 1], [2, 3], [4, 5], [6, 7]],
            ins=[cc_in.opt()], outs=[cc_out.opt()]).then_inc(scc.h, 1)
        g.wait_ge(scc.h, 1)
        g.dma_start(out=carr[0][:], in_=cc_out[0:128, :]).then_inc(scr.h, 16)
        g.dma_start(out=carr[1][:], in_=cc_out[128:256, :]).then_inc(scr.h, 16)
        for c in range(NT):
            for f in range(ND):
                bcf = c * ND + f
                if bcf >= 2:
                    g.wait_ge(sbv.h, bcf - 1)
                g.dma_start(out=rl[bcf % 2][:, 0:TC],
                            in_=kvc_d[f][:, tc_(c)]).then_inc(sbl.h, 16)
                g.dma_start(out=rl[bcf % 2][:, TC:2 * TC],
                            in_=kc_d[f][:, tc_(c)]).then_inc(sbl.h, 16)
                g.dma_start(out=rl[bcf % 2][:, 2 * TC:3 * TC],
                            in_=qp_d[f][:, tc_(c)]).then_inc(sbl.h, 16)
            for dout in range(ND):
                gd = c * ND + dout
                if gd >= 2:
                    g.wait_ge(smo.h, gd - 1)
                g.dma_start(out=wo_b[gd % 2][:], in_=wout[dout]).then_inc(swo.h, 16)

    def f_scalar(s):
        s.wait_ge(com.init.h, Common.READY)
        for c in range(NT):
            s.wait_ge(sx.h, 256 * (c + 1))
            ln.scalar_chunk(s, c, x_sb, xcol)
        def emit_load(fc):
            c = fc % NT
            s.wait_ge(sxnst.h, 256 * (c + 1))
            if fc >= 2:
                s.wait_ge(sma.h, 3 * (fc - 2) + 3)
            s.dma_start(out=xnc[fc % 2][:], in_=xn_d[c]).then_inc(sxnl.h, 16)

        def emit_exps(fc):
            c = fc % NT
            if fc >= 2:
                s.wait_ge(sfree.h, 3 * fc - 4)
            s.wait_ge(sp1.h, 2 * fc + 1)
            nc.scalar.activation(phi_t[c % 2][:], phi_t[c % 2][:],
                                 AF.Exp).then_inc(sp2.h, 1)
            s.wait_ge(sp1.h, 2 * fc + 2)
            nc.scalar.activation(phi_t[2 + c % 2][:], phi_t[2 + c % 2][:],
                                 AF.Exp).then_inc(sp2.h, 1)

        total_fc = ND * NT
        emit_load(0)
        for fc in range(1, total_fc):
            emit_load(fc)
            emit_exps(fc - 1)
        emit_exps(total_fc - 1)

    def f_tensor(t):
        t.wait_ge(com.init.h, Common.READY)
        for c in range(NT):
            t.wait_ge(sx.h, 256 * (c + 1))
            ln.tensor_chunk(t, c, x_sb, xcol, PS[0], PS[1])
        for f in range(ND):
            for c in range(NT):
                fc = f * NT + c
                t.wait_ge(sxnl.h, 16 * (fc + 1))
                t.wait_ge(swq.h, 48 * f + 48)
                if fc >= 2:
                    t.wait_ge(sfree.h, 3 * fc - 3)
                pk, pv, pq = PS[2 + fc % 2], PS[4 + fc % 2], PS[6 + fc % 2]
                for kk in range(ND):
                    mm = nc.tensor.matmul(pk[:], wbk[f % 2][:, kk * 128:(kk + 1) * 128],
                                          xnc[fc % 2][:, xcol(kk)],
                                          start=(kk == 0), stop=(kk == ND - 1))
                mm.then_inc(sma.h, 1)
                for kk in range(ND):
                    mm = nc.tensor.matmul(pv[:], wbv[f % 2][:, kk * 128:(kk + 1) * 128],
                                          xnc[fc % 2][:, xcol(kk)],
                                          start=(kk == 0), stop=(kk == ND - 1))
                mm.then_inc(sma.h, 1)
                for kk in range(ND):
                    mm = nc.tensor.matmul(pq[:], wbq[f % 2][:, kk * 128:(kk + 1) * 128],
                                          xnc[fc % 2][:, xcol(kk)],
                                          start=(kk == 0), stop=(kk == ND - 1))
                mm.then_inc(sma.h, 1)
        for c in range(NT):
            t.wait_ge(sbv.h, (c + 1) * ND)
            for dout in range(ND):
                gd = c * ND + dout
                t.wait_ge(swo.h, 16 * (gd + 1))
                if gd >= 2:
                    t.wait_ge(sob.h, gd - 1)
                pd = PS[2 + gd % 2]
                for kk in range(ND):
                    mm = nc.tensor.matmul(pd[:], wo_b[gd % 2][:, kk * 128:(kk + 1) * 128],
                                          att_sb[:, xcol(kk)],
                                          start=(kk == 0), stop=(kk == ND - 1))
                mm.then_inc(smo.h, 1)

    def f_vector(v):
        v.wait_ge(com.init.h, Common.READY)
        nc.vector.memset(ones_tc[:], 1.0)
        for c in range(NT):
            ln.vector_chunk(v, c, PS[0], PS[1])
            v.wait_ge(ln.s_bc.h, c + 1)
            if c >= 1:
                v.wait_ge(sxnst.h, 256 * c)
            emit_normalize(nc, v, snrm, snrm2, c * ND, x_sb, xcol, xn_sb,
                           xcol, PS[0], PS[1], sub3=sub3)
        for f in range(ND):
            for c in range(NT):
                fc = f * NT + c
                if c == 0 and f >= 1:
                    v.wait_ge(ssta.h, 48 * f)
                pk, pv, pq = PS[2 + fc % 2], PS[4 + fc % 2], PS[6 + fc % 2]
                if fc >= 2:
                    v.wait_ge(sp2.h, 2 * fc - 2)
                v.wait_ge(sma.h, 3 * fc + 1)
                nc.vector.tensor_single_scalar(phi_t[c % 2][:], pk[:], 0.0,
                                               ALU.min).then_inc(sp1.h, 1)
                v.wait_ge(sma.h, 3 * fc + 3)
                nc.vector.tensor_single_scalar(phi_t[2 + c % 2][:], pq[:], 0.0,
                                               ALU.min).then_inc(sp1.h, 1)
                v.wait_ge(sp2.h, 2 * fc + 1)
                nc.vector.scalar_tensor_tensor(kp_sb[:, tc_(c)], pk[:], 0.0,
                                               phi_t[c % 2][:], ALU.max, ALU.add
                                               ).then_inc(sfree.h, 1)
                v.wait_ge(sp2.h, 2 * fc + 2)
                nc.vector.scalar_tensor_tensor(qp_par[0][:, tc_(c)], pq[:], 0.0,
                                               phi_t[2 + c % 2][:], ALU.max, ALU.add
                                               ).then_inc(sfree.h, 1)
                v.wait_ge(sfree.h, 3 * fc + 2)
                nc.vector.tensor_tensor(kv_sb[:, tc_(c)], kp_sb[:, tc_(c)], pv[:],
                                        ALU.mult).then_inc(sfree.h, 1)
            v.wait_ge(sfree.h, 12 * f + 10)
            nc.vector.tensor_tensor_scan(kc_par[0][:], ones_tc[:], kp_sb[:],
                                         0.0, ALU.mult, ALU.add).then_inc(ssc2.h, 1)
            v.wait_ge(sfree.h, 12 * f + 12)
            nc.vector.tensor_tensor_scan(kvc_par[0][:], ones_tc[:], kv_sb[:],
                                         0.0, ALU.mult, ALU.add).then_inc(ssc2.h, 1)
            v.wait_ge(ssc2.h, 4 * f + 1)
            nc.vector.tensor_copy(totals[:, f:f + 1], kc_par[0][:, TOK - 1:TOK]
                                  ).then_inc(ssc2.h, 1)
            v.wait_ge(ssc2.h, 4 * f + 2)
            nc.vector.tensor_copy(totals[:, 16 + f:17 + f],
                                  kvc_par[0][:, TOK - 1:TOK]).then_inc(ssc2.h, 1)
        v.wait_ge(scsl.h, 16)
        v.wait_ge(scr.h, 32)
        nc.vector.tensor_scalar_mul(carry[:], carr[0][:], csel[:, 0:1]).then_inc(scar.h, 1)
        nc.vector.tensor_scalar_mul(ct2[:], carr[1][:], csel[:, 1:2]).then_inc(scar.h, 1)
        v.wait_ge(scar.h, 2)
        nc.vector.tensor_add(carry[:], carry[:], ct2[:]).then_inc(scar.h, 1)
        for c in range(NT):
            if c >= 1:
                v.wait_ge(smo.h, c * ND)
            for f in range(ND):
                bcf = c * ND + f
                b = rl[bcf % 2]
                a1 = sc4[:, 0:TC]
                a2 = sc4[:, TC:2 * TC]
                num = sc4[:, 2 * TC:3 * TC]
                den = sc4[:, 3 * TC:4 * TC]
                v.wait_ge(sbl.h, 48 * (bcf + 1))
                if bcf >= 1:
                    v.wait_ge(sbv.h, bcf)
                nc.vector.tensor_scalar_add(a1, b[:, 0:TC], carry[:, 16 + f:17 + f]
                                            ).then_inc(sbs.h, 1)
                nc.vector.tensor_scalar_add(a2, b[:, TC:2 * TC], carry[:, f:f + 1]
                                            ).then_inc(sbs.h, 1)
                v.wait_ge(sbs.h, 6 * bcf + 1)
                nc.vector.tensor_tensor(num, b[:, 2 * TC:3 * TC], a1, ALU.mult
                                        ).then_inc(sbs.h, 1)
                v.wait_ge(sbs.h, 6 * bcf + 2)
                nc.vector.tensor_tensor(den, b[:, 2 * TC:3 * TC], a2, ALU.mult
                                        ).then_inc(sbs.h, 1)
                v.wait_ge(sbs.h, 6 * bcf + 4)
                nc.vector.tensor_scalar_add(den, den, ATT_EPS).then_inc(sbs.h, 1)
                v.wait_ge(sbs.h, 6 * bcf + 5)
                nc.vector.reciprocal(a1, den).then_inc(sbs.h, 1)
                v.wait_ge(sbs.h, 6 * bcf + 6)
                nc.vector.scalar_tensor_tensor(att_sb[:, xcol(f)], num, ATT_SCALE,
                                               a1, ALU.mult, ALU.mult
                                               ).then_inc(sbv.h, 1)
            for dout in range(ND):
                gd = c * ND + dout
                v.wait_ge(smo.h, gd + 1)
                if gd >= 2:
                    v.wait_ge(sst.h, 16 * (gd - 1))
                nc.vector.tensor_add(ob[gd % 2][:], x_sb[:, xcol(dout)],
                                     PS[2 + gd % 2][:]).then_inc(sob.h, 1)

    return dict(sync=f_sync, gpsimd=f_gpsimd, tensor=f_tensor,
                scalar=f_scalar, vector=f_vector,
                done=(sst, 16 * ND * NT))



TCF = 256
NTF = TOK // TCF
NG = 32


def emit_ffn_genes(p: Prog, com: Common, tag: str, x_in, x_out, wc, genes_d,
                   wdwu_d, normw_d, tscal_d, prev=None):
    nc = p.nc
    ln = LNCore(p, com, tag + "L", rms_eps=float(np.finfo(np.float32).eps),
                tc=TCF)
    xcol = lambda k: slice(k * TCF, (k + 1) * TCF)

    x_sb = p.sbuf([128, ND * TCF], F32, tag + "x")
    xnw_f = p.sbuf([128, ND * TCF], F32, tag + "xnf")
    xnw_b = p.sbuf([128, ND * TCF], BF16, tag + "xnb")
    sub3 = p.sbuf([128, 3 * TCF], F32, tag + "sub")
    wcb = [p.sbuf([128, D], BF16, f"{tag}wc{i}") for i in range(2)]
    logit = p.sbuf([128, NG * TCF], F32, tag + "lg")
    lt = p.sbuf([128, 4096], F32, tag + "lt")
    lt2 = p.sbuf([128, 4096], F32, tag + "lt2")
    m8 = p.sbuf([128, 32], F32, tag + "m8")
    mt_rows = p.sbuf([1, 2 * TCF], F32, tag + "mtr")
    wg_sb = p.sbuf([128, NG * TCF], BF16, tag + "w")
    esc = p.sbuf([128, TCF], F32, tag + "esc")
    ind_f = p.sbuf([128, TCF], F32, tag + "ind")
    genes_sb = p.sbuf([128, NG * 257], BF16, tag + "gen")
    expr_f = p.sbuf([128, 257], F32, tag + "exf")
    expr_nf = p.sbuf([128, 256], F32, tag + "exn")
    zrec = p.sbuf([128, 1], F32, tag + "zr")
    exprT = p.sbuf([128, 2 * TCF], BF16, tag + "exT")
    dwn = [p.sbuf([128, TCF], F32, f"{tag}dw{i}") for i in range(2)]
    prd = [p.sbuf([128, TCF], F32, f"{tag}pr{i}") for i in range(2)]
    ob = [p.sbuf([128, TCF], F32, f"{tag}ob{i}") for i in range(2)]
    normw = p.sbuf([128, ND], F32, tag + "nw")
    tsc = p.sbuf([1, 2], F32, tag + "ts")
    invt_col = p.sbuf([128, 1], F32, tag + "it")
    scal_col = p.sbuf([128, 1], F32, tag + "sc")
    wd_sb = p.sbuf([128, ND * 256], BF16, tag + "wdb")
    wu_sb = p.sbuf([128, ND * 256], BF16, tag + "wub")
    PS = [p.psum([128, 512], F32, f"{tag}ps{i}") for i in range(8)]

    sx = p.sem(tag + "sx")
    sini = p.sem(tag + "sini")
    sini2 = p.sem(tag + "sini2")
    sgm = p.sem(tag + "sgm")
    srw = p.sem(tag + "srw")
    snA = p.sem(tag + "snA")
    snB = p.sem(tag + "snB")
    snC = p.sem(tag + "snC")
    snD = p.sem(tag + "snD")
    swc = p.sem(tag + "swc")
    sml = p.sem(tag + "sml")
    slev = p.sem(tag + "slev")
    smt = p.sem(tag + "smt")
    stev = p.sem(tag + "stev")
    stk = p.sem(tag + "stk")
    smr = p.sem(tag + "smr")
    sbc = p.sem(tag + "sbc")
    sw = p.sem(tag + "sw")
    sme = p.sem(tag + "sme")
    sev = p.sem(tag + "sev")
    smx = p.sem(tag + "smx")
    sxv = p.sem(tag + "sxv")
    smd = p.sem(tag + "smd")
    sth = p.sem(tag + "sth")
    spr = p.sem(tag + "spr")
    sob = p.sem(tag + "sob")
    sst = p.sem(tag + "sst")

    NSUB = 2 * NTF
    TKS = 7

    def f_sync(sync):
        if prev is not None:
            sync.wait_ge(prev[0].h, prev[1])
        sync.dma_start(out=tsc[:], in_=tscal_d).then_inc(sini.h, 16)
        sync.dma_start(out=normw[:], in_=normw_d).then_inc(sini.h, 16)
        for c in range(NTF):
            if c >= 1:
                sync.wait_ge(sob.h, c * ND)
            for k in range(ND):
                sync.dma_start(out=x_sb[:, xcol(k)],
                               in_=x_in[k * 128:(k + 1) * 128, c * TCF:(c + 1) * TCF]
                               ).then_inc(sx.h, 16)
            for j in range(ND):
                g = c * ND + j
                sync.wait_ge(sob.h, g + 1)
                sync.dma_start(out=x_out[j * 128:(j + 1) * 128, c * TCF:(c + 1) * TCF],
                               in_=ob[j % 2][:]).then_inc(sst.h, 16)
        sync.wait_ge(sst.h, 16 * ND * NTF)

    def f_gpsimd(g):
        for gb in range(NG):
            g.dma_start(out=genes_sb[:, gb * 257:gb * 257 + 256],
                        in_=genes_d[gb]).then_inc(sini2.h, 16)
        g.dma_start(out=wd_sb[:], in_=wdwu_d[0]).then_inc(sini2.h, 16)
        g.dma_start(out=wu_sb[:], in_=wdwu_d[1]).then_inc(sini2.h, 16)
        for c in range(NTF):
            for gb in range(NG):
                i = c * NG + gb
                if i >= 2:
                    g.wait_ge(sml.h, i - 1)
                g.dma_start(out=wcb[i % 2][:], in_=wc[gb]).then_inc(swc.h, 16)

    def f_tensor(t):
        t.wait_ge(com.init.h, Common.READY)
        t.wait_ge(srw.h, 2)
        nc.tensor.matmul(PS[6][0:128, 0:1], com.ones_row[:], tsc[0:1, 0:1],
                         start=True, stop=True)
        nc.tensor.matmul(PS[7][0:128, 0:1], com.ones_row[:], tsc[0:1, 1:2],
                         start=True, stop=True).then_inc(srw.h, 1)
        t.wait_ge(srw.h, 5)
        for c in range(NTF):
            t.wait_ge(sx.h, 256 * (c + 1))
            if c >= 1:
                t.wait_ge(sth.h, c * ND)
            ln.tensor_chunk(t, c, x_sb, xcol, PS[0], PS[1])
            t.wait_ge(snD.h, ND * (c + 1))
            t.wait_ge(sgm.h, 1)
            t.wait_ge(sini2.h, 16 * (2 + NG))
            for gb in range(NG):
                i = c * NG + gb
                t.wait_ge(swc.h, 16 * (i + 1))
                if i >= 2:
                    t.wait_ge(slev.h, i - 1)
                pg = PS[2 + i % 2]
                for k in range(ND):
                    mm = nc.tensor.matmul(pg[:, 0:TCF],
                                          wcb[i % 2][:, k * 128:(k + 1) * 128],
                                          xnw_b[:, xcol(k)],
                                          start=(k == 0), stop=(k == ND - 1))
                mm.then_inc(sml.h, 1)
            for ts_i in range(2):
                sub = c * 2 + ts_i
                for gb in range(NG):
                    j = sub * NG + gb
                    t.wait_ge(slev.h, c * NG + gb + 1)
                    if j >= 2:
                        t.wait_ge(stev.h, j - 1)
                    nc.tensor.transpose(
                        PS[4 + j % 2][0:128, 0:128],
                        logit[:, gb * TCF + ts_i * 128:gb * TCF + (ts_i + 1) * 128],
                        com.ident_f[:]).then_inc(smt.h, 1)
                t.wait_ge(stk.h, TKS * (sub + 1))
                t.wait_ge(smr.h, 2 * sub)
                nc.tensor.matmul(PS[6][0:1, 256:384], m8[:, 0:1], com.ident_f[:],
                                 start=True, stop=True)
                nc.tensor.matmul(PS[7][0:1, 256:384], m8[:, 31:32], com.ident_f[:],
                                 start=True, stop=True).then_inc(smr.h, 1)
            t.wait_ge(smr.h, 4 * c + 4)
            nc.tensor.matmul(PS[6][:, 0:TCF], com.ones_row[:], mt_rows[0:1, 0:TCF],
                             start=True, stop=True)
            nc.tensor.matmul(PS[7][:, 0:TCF], com.ones_row[:],
                             mt_rows[0:1, TCF:2 * TCF],
                             start=True, stop=True).then_inc(sbc.h, 1)
            t.wait_ge(sw.h, 4 * NG * (c + 1))
            for ts_i in range(2):
                sub = c * 2 + ts_i
                if sub >= 2:
                    t.wait_ge(sev.h, 3 * (sub - 1))
                pe_ = PS[2 + ts_i]
                for gb in range(NG):
                    mm = nc.tensor.matmul(
                        pe_[:, 0:257],
                        wg_sb[:, gb * TCF + ts_i * 128:gb * TCF + (ts_i + 1) * 128],
                        genes_sb[:, gb * 257:(gb + 1) * 257],
                        start=(gb == 0), stop=(gb == NG - 1))
                mm.then_inc(sme.h, 1)
                t.wait_ge(sev.h, 3 * sub + 3)
                if sub >= 2:
                    t.wait_ge(sxv.h, 2 * (sub - 1))
                for cc in range(2):
                    nc.tensor.transpose(PS[4 + cc][0:128, 128:256],
                                        expr_nf[:, cc * 128:(cc + 1) * 128],
                                        com.ident_f[:]).then_inc(smx.h, 1)
            t.wait_ge(sxv.h, 2 * (2 * c + 2))
            for dout in range(ND):
                gd = c * ND + dout
                if gd >= 2:
                    t.wait_ge(sth.h, gd - 1)
                    t.wait_ge(spr.h, 2 * (gd - 1))
                pd = PS[gd % 2]
                pu = PS[2 + gd % 2]
                for cc in range(2):
                    nc.tensor.matmul(pd[:, 0:TCF],
                                     wd_sb[:, dout * 256 + cc * 128:dout * 256 + (cc + 1) * 128],
                                     exprT[:, cc * TCF:(cc + 1) * TCF],
                                     start=(cc == 0), stop=(cc == 1))
                for cc in range(2):
                    mm = nc.tensor.matmul(pu[:, 0:TCF],
                                          wu_sb[:, dout * 256 + cc * 128:dout * 256 + (cc + 1) * 128],
                                          exprT[:, cc * TCF:(cc + 1) * TCF],
                                          start=(cc == 0), stop=(cc == 1))
                mm.then_inc(smd.h, 1)

    def f_scalar(s):
        s.wait_ge(com.init.h, Common.READY)
        for c in range(NTF):
            s.wait_ge(sx.h, 256 * (c + 1))
            ln.scalar_chunk(s, c, x_sb, xcol)
            if c >= 1:
                s.wait_ge(sw.h, 4 * NG * c)
            for gb in range(NG):
                i = c * NG + gb
                s.wait_ge(sml.h, i + 1)
                nc.scalar.copy(logit[:, gb * TCF:(gb + 1) * TCF],
                               PS[2 + i % 2][:, 0:TCF]).then_inc(slev.h, 1)
            for ts_i in range(2):
                sub = c * 2 + ts_i
                if sub >= 1:
                    s.wait_ge(stk.h, TKS * sub)
                for gb in range(NG):
                    j = sub * NG + gb
                    s.wait_ge(smt.h, j + 1)
                    nc.scalar.copy(lt[:, gb * 128:(gb + 1) * 128],
                                   PS[4 + j % 2][0:128, 0:128]).then_inc(stev.h, 1)
                s.wait_ge(smr.h, 2 * sub + 1)
                nc.scalar.copy(mt_rows[0:1, ts_i * 128:(ts_i + 1) * 128],
                               PS[6][0:1, 256:384])
                nc.scalar.copy(mt_rows[0:1, TCF + ts_i * 128:TCF + (ts_i + 1) * 128],
                               PS[7][0:1, 256:384]).then_inc(smr.h, 1)
            for gb in range(NG):
                iw = (c * NG + gb) * 4
                s.wait_ge(sw.h, iw + 1)
                nc.scalar.activation(esc[:], esc[:], AF.Exp, scale=invt_col[:]
                                     ).then_inc(sw.h, 1)
            for dout in range(ND):
                gd = c * ND + dout
                s.wait_ge(smd.h, gd + 1)
                if gd >= 2:
                    s.wait_ge(spr.h, 2 * (gd - 1) + 1)
                nc.scalar.activation(dwn[gd % 2][:], PS[gd % 2][:, 0:TCF],
                                     AF.Tanh).then_inc(sth.h, 1)

    def f_vector(v):
        v.wait_ge(com.init.h, Common.READY)
        for gb in range(NG):
            nc.vector.memset(genes_sb[:, gb * 257 + 256:gb * 257 + 257], 1.0)
        nc.vector.engine_nop()
        nc.vector.memset(m8[:, 0:1], 0.0).then_inc(sgm.h, 1)
        v.wait_ge(sini.h, 32)
        nc.vector.tensor_scalar_max(tsc[0:1, 0:1], tsc[0:1, 0:1], 0.1).then_inc(srw.h, 1)
        v.wait_ge(srw.h, 1)
        nc.vector.reciprocal(tsc[0:1, 0:1], tsc[0:1, 0:1]).then_inc(srw.h, 1)
        v.wait_ge(srw.h, 3)
        nc.vector.tensor_copy(invt_col[:], PS[6][:, 0:1]).then_inc(srw.h, 1)
        v.wait_ge(srw.h, 4)
        nc.vector.tensor_copy(scal_col[:], PS[7][:, 0:1]).then_inc(srw.h, 1)
        for c in range(NTF):
            ln.vector_chunk(v, c, PS[0], PS[1])
            v.wait_ge(ln.s_bc.h, c + 1)
            if c >= 1:
                v.wait_ge(sml.h, c * NG)
            for k in range(ND + 6):
                if k < ND:
                    if k >= 3:
                        v.wait_ge(snB.h, c * ND + k - 2)
                    nc.vector.tensor_tensor(sub3[:, (k % 3) * TCF:(k % 3 + 1) * TCF],
                                            x_sb[:, xcol(k)], PS[0][:, 0:TCF],
                                            ALU.subtract).then_inc(snA.h, 1)
                if 2 <= k < ND + 2:
                    j = k - 2
                    v.wait_ge(snA.h, c * ND + j + 1)
                    nc.vector.tensor_tensor(xnw_f[:, xcol(j)],
                                            sub3[:, (j % 3) * TCF:(j % 3 + 1) * TCF],
                                            PS[1][:, 0:TCF], ALU.mult).then_inc(snB.h, 1)
                if 4 <= k < ND + 4:
                    j = k - 4
                    v.wait_ge(snB.h, c * ND + j + 1)
                    nc.vector.tensor_scalar_mul(xnw_f[:, xcol(j)], xnw_f[:, xcol(j)],
                                                normw[:, j:j + 1]).then_inc(snC.h, 1)
                if k >= 6:
                    j = k - 6
                    v.wait_ge(snC.h, c * ND + j + 1)
                    nc.vector.tensor_copy(xnw_b[:, xcol(j)], xnw_f[:, xcol(j)]
                                          ).then_inc(snD.h, 1)
            for ts_i in range(2):
                sub = c * 2 + ts_i
                v.wait_ge(stev.h, (sub + 1) * NG)
                if sub >= 1:
                    v.wait_ge(smr.h, 2 * sub - 1)
                src, dst = lt, lt2
                for rnd in range(4):
                    st0 = TKS * sub + 2 * rnd
                    if rnd > 0:
                        v.wait_ge(stk.h, st0)
                    nc.vector.max(out=m8[:, rnd * 8:(rnd + 1) * 8], in_=src[:]
                                  ).then_inc(stk.h, 1)
                    if rnd < 3:
                        v.wait_ge(stk.h, st0 + 1)
                        nc.vector.match_replace(out=dst[:],
                                                in_to_replace=m8[:, rnd * 8:(rnd + 1) * 8],
                                                in_values=src[:], imm_value=-1e30
                                                ).then_inc(stk.h, 1)
                    src, dst = dst, src
            v.wait_ge(sbc.h, c + 1)
            for gb in range(NG):
                iw = (c * NG + gb) * 4
                if c * NG + gb >= 1:
                    v.wait_ge(sw.h, iw)
                nc.vector.tensor_tensor(esc[:], logit[:, gb * TCF:(gb + 1) * TCF],
                                        PS[6][:, 0:TCF], ALU.subtract).then_inc(sw.h, 1)
                nc.vector.tensor_tensor(ind_f[:], logit[:, gb * TCF:(gb + 1) * TCF],
                                        PS[7][:, 0:TCF], ALU.is_ge).then_inc(sw.h, 1)
                v.wait_ge(sw.h, iw + 3)
                nc.vector.tensor_tensor(wg_sb[:, gb * TCF:(gb + 1) * TCF],
                                        esc[:], ind_f[:], ALU.mult).then_inc(sw.h, 1)
            for ts_i in range(2):
                sub = c * 2 + ts_i
                v.wait_ge(sme.h, sub + 1)
                if sub >= 1:
                    v.wait_ge(smx.h, 2 * sub)
                pe_ = PS[2 + ts_i]
                nc.vector.tensor_copy(expr_f[:], pe_[:, 0:257]).then_inc(sev.h, 1)
                v.wait_ge(sev.h, 3 * sub + 1)
                nc.vector.reciprocal(zrec[:], expr_f[:, 256:257]).then_inc(sev.h, 1)
                v.wait_ge(sev.h, 3 * sub + 2)
                nc.vector.tensor_scalar_mul(expr_nf[:], expr_f[:, 0:256], zrec[:]
                                            ).then_inc(sev.h, 1)
                for cc in range(2):
                    v.wait_ge(smx.h, sub * 2 + cc + 1)
                    nc.vector.tensor_copy(
                        exprT[:, cc * TCF + ts_i * 128:cc * TCF + ts_i * 128 + 128],
                        PS[4 + cc][0:128, 128:256]).then_inc(sxv.h, 1)
            for dout in range(ND):
                gd = c * ND + dout
                ip = 2 * gd
                v.wait_ge(sth.h, gd + 1)
                nc.vector.tensor_tensor(prd[gd % 2][:], dwn[gd % 2][:],
                                        xnw_f[:, xcol(dout)], ALU.mult).then_inc(spr.h, 1)
                v.wait_ge(spr.h, ip + 1)
                nc.vector.tensor_tensor(prd[gd % 2][:], prd[gd % 2][:],
                                        PS[2 + gd % 2][:, 0:TCF], ALU.mult).then_inc(spr.h, 1)
                v.wait_ge(spr.h, ip + 2)
                if gd >= 2:
                    v.wait_ge(sst.h, 16 * (gd - 1))
                nc.vector.scalar_tensor_tensor(ob[gd % 2][:], prd[gd % 2][:],
                                               scal_col[:], x_sb[:, xcol(dout)],
                                               ALU.mult, ALU.add).then_inc(sob.h, 1)

    return dict(sync=f_sync, gpsimd=f_gpsimd, tensor=f_tensor,
                scalar=f_scalar, vector=f_vector,
                done=(sst, 16 * ND * NTF))




def emit_attn_gqa(p: Prog, com: Common, tag: str, x_in, x_out, wq, wk, wv_img,
                  wo, ropes_d, maskv_d, trimask_d, prev=None):
    nc = p.nc
    xn_d = [p.dram(f"{tag}xnd{c}", [128, ND * TC], BF16) for c in range(NT)]
    q_d = p.dram(tag + "qd", [16, 128, TOK], BF16)
    cc_in = p.dram(tag + "ccin", [128, 16384], BF16)
    cc_out = p.dram(tag + "ccout", [256, 16384], BF16)

    ln = LNCore(p, com, tag + "L")
    k_own = p.sbuf([128, 4 * TOK], BF16, tag + "ko")
    v_own = p.sbuf([128, 16 * 512], BF16, tag + "vo")
    wv_sb = p.sbuf([128, ND * 512], BF16, tag + "wv")
    maskv = p.sbuf([128, NT * 32], F32, tag + "mkv")
    trimask = p.sbuf([128, 4 * TC], F32, tag + "tri")
    qbf = [p.sbuf([128, TC], BF16, f"{tag}qb{i}") for i in range(2)]
    A = Arena(p, tag + "ar", 28500)
    x_sb = A.f32(ND * TC)
    xn_sb = A.bf16(ND * TC // 2)
    sub3 = A.f32(3 * TC)
    markB = A.cur
    xnc = [A.bf16(ND * TC // 2) for i in range(2)]
    rck = A.f32(TC)
    rsk = A.f32(TC)
    rcq = A.f32(TC)
    rsq = A.f32(TC)
    rcs = A.f32(TC)
    rtmp = A.f32(TC)
    wblk = [A.bf16(D // 2) for i in range(2)]
    A.reset(0)
    pT = A.bf16(36 * 512 // 2)
    o_all = A.bf16(ND * 512 // 2)
    kbuf = [A.bf16(SEQ // 2) for i in range(2)]
    vbuf = [A.bf16(SEQ // 2) for i in range(2)]
    qh = [A.bf16(256) for i in range(2)]
    pq = A.bf16(256)
    intra_s = A.f32(512)
    of = A.f32(512)
    zrow = A.f32(512)
    woblk = [A.bf16(D // 2) for i in range(2)]
    xr = [A.f32(512) for i in range(2)]
    ob = [A.f32(512) for i in range(2)]
    PS = [p.psum([128, 512], F32, f"{tag}ps{i}") for i in range(7)]
    PSB = p.psum([128, 1024], BF16, tag + "psb")

    sx = p.sem(tag + "sx")
    sxnst = p.sem(tag + "sxnst")
    sxnl = p.sem(tag + "sxnl")
    snrm = p.sem(tag + "snrA")
    snrm2 = p.sem(tag + "snrB")
    srt = p.sem(tag + "srt")
    swp = p.sem(tag + "swp")
    swi = p.sem(tag + "swi")
    smb = p.sem(tag + "smb")
    svv = p.sem(tag + "svv")
    srp = p.sem(tag + "srp")
    sqst = p.sem(tag + "sqst")
    skst = p.sem(tag + "skst")
    scc = p.sem(tag + "scc")
    skl = p.sem(tag + "skl")
    sql = p.sem(tag + "sql")
    sms = p.sem(tag + "sms")
    sxp = p.sem(tag + "sxp")
    smi = p.sem(tag + "smi")
    stp = p.sem(tag + "stp")
    stv = p.sem(tag + "stv")
    sza = p.sem(tag + "sza")
    soz = p.sem(tag + "soz")
    swo = p.sem(tag + "swo")
    sxr = p.sem(tag + "sxr")
    smo = p.sem(tag + "smo")
    sob = p.sem(tag + "sob")
    sst = p.sem(tag + "sst")

    xcol = lambda k: slice(k * TC, (k + 1) * TC)
    NGRP = 24
    RPC = 100

    def rp_base(c, gi):
        if gi < 4:
            return RPC * c + 5 * gi
        return RPC * c + 20 + 5 * (gi - 8)

    evt = {}
    for c in range(NT):
        for gi in range(NGRP):
            i = c * NGRP + gi
            if gi < 4 or gi >= 8:
                evt[i] = (srp, rp_base(c, gi) + 3)
            else:
                evt[i] = (svv, c * 4 + (gi - 4) + 1)

    def f_sync(sync):
        if prev is not None:
            sync.wait_ge(prev[0].h, prev[1])
        for c in range(NT):
            for k in range(ND):
                sync.dma_start(out=x_sb[:, xcol(k)],
                               in_=x_in[k * 128:(k + 1) * 128, c * TC:(c + 1) * TC]
                               ).then_inc(sx.h, 16)
            sync.wait_ge(snrm2.h, ND * (c + 1))
            for k in range(ND):
                sync.dma_start(out=xn_d[c][:, xcol(k)], in_=xn_sb[:, xcol(k)]
                               ).then_inc(sxnst.h, 16)
        for c in range(NT):
            for kvh in range(4):
                sync.wait_ge(srp.h, RPC * c + 5 * (kvh + 1))
                sync.dma_start(
                    out=cc_in[:, kvh * 2048 + c * TC:kvh * 2048 + (c + 1) * TC],
                    in_=k_own[:, kvh * TOK + c * TC:kvh * TOK + (c + 1) * TC]
                    ).then_inc(skst.h, 16)
            for j in range(4):
                sync.wait_ge(svv.h, c * 4 + j + 1)
                for kvh in range(4):
                    sync.dma_start(
                        out=cc_in[:, 8192 + kvh * 2048 + (c * 4 + j) * 128:
                                  8192 + kvh * 2048 + (c * 4 + j + 1) * 128],
                        in_=v_own[:, (c * 4 + j) * 512 + kvh * 128:
                                  (c * 4 + j) * 512 + (kvh + 1) * 128]
                        ).then_inc(skst.h, 16)
            for h in range(16):
                sync.wait_ge(srp.h, RPC * c + 20 + 5 * (h + 1))
                sync.dma_start(out=q_d[h][:, c * TC:(c + 1) * TC],
                               in_=qbf[(c * 16 + h) % 2][:]).then_inc(sqst.h, 16)
        for g in range(NT):
            for dout in range(ND):
                gd = g * ND + dout
                if gd >= 2:
                    sync.wait_ge(sob.h, gd - 1)
                sync.dma_start(out=xr[gd % 2][:],
                               in_=x_in[dout * 128:(dout + 1) * 128,
                                        g * TC:(g + 1) * TC]).then_inc(sxr.h, 16)
                sync.wait_ge(sob.h, gd + 1)
                sync.dma_start(out=x_out[dout * 128:(dout + 1) * 128,
                                         g * TC:(g + 1) * TC],
                               in_=ob[gd % 2][:]).then_inc(sst.h, 16)
        sync.wait_ge(sst.h, 16 * ND * NT)

    def f_scalar(s):
        s.wait_ge(com.init.h, Common.READY)
        for c in range(NT):
            s.wait_ge(sx.h, 256 * (c + 1))
            ln.scalar_chunk(s, c, x_sb, xcol)
        for c in range(NT):
            s.wait_ge(sxnst.h, 256 * (c + 1))
            if c >= 2:
                s.wait_ge(smb.h, NGRP * (c - 1))
            s.dma_start(out=xnc[c % 2][:], in_=xn_d[c]).then_inc(sxnl.h, 16)
            for j in range(4):
                i = c * NGRP + 4 + j
                s.wait_ge(smb.h, i + 1)
                nc.scalar.copy(v_own[:, (c * 4 + j) * 512:(c * 4 + j + 1) * 512],
                               PS[2 + i % 2][:]).then_inc(svv.h, 1)
        for g in range(NT):
            for h in range(16):
                hg = g * 16 + h
                s.wait_ge(sza.h, hg)
                for kc in range(32):
                    s.wait_ge(sms.h, 32 * hg + kc + 1)
                    nc.scalar.activation(
                        pT[:, kc * 512:(kc + 1) * 512], PS[2 + kc % 2][:],
                        AF.Exp, bias=maskv[:, g * 32 + kc:g * 32 + kc + 1]
                        ).then_inc(sxp.h, 1)
                for i in range(4):
                    if i >= 1:
                        s.wait_ge(stp.h, 16 * hg + 4 * i)
                    s.wait_ge(smi.h, 12 * hg + 3 * i + 2)
                    nc.scalar.activation(pq[:], intra_s[:], AF.Exp
                                         ).then_inc(smi.h, 1)

    def f_gpsimd(g):
        g.dma_start(out=maskv[:], in_=maskv_d).then_inc(srt.h, 16)
        g.dma_start(out=trimask[:], in_=trimask_d).then_inc(srt.h, 16)
        g.dma_start(out=wv_sb[:], in_=wv_img).then_inc(swi.h, 16)
        for c in range(NT):
            if c >= 1:
                g.wait_ge(srp.h, RPC * c)
            for t_i, buf in enumerate([rck, rsk, rcq, rsq]):
                g.dma_start(out=buf[:],
                            in_=ropes_d[t_i * 128:(t_i + 1) * 128,
                                        c * TC:(c + 1) * TC]).then_inc(srt.h, 16)
            for kvh in range(4):
                i = c * NGRP + kvh
                if i >= 2:
                    g.wait_ge(smb.h, i - 1)
                g.dma_start(out=wblk[i % 2][:], in_=wk[kvh]).then_inc(swp.h, 16)
            for h in range(16):
                i = c * NGRP + 8 + h
                if i >= 2:
                    g.wait_ge(smb.h, i - 1)
                g.dma_start(out=wblk[i % 2][:], in_=wq[h]).then_inc(swp.h, 16)
        g.wait_ge(skst.h, 16 * 8 * NT)
        nc.gpsimd.collective_compute(
            "AllGather", ALU.bypass,
            replica_groups=[[0, 1], [2, 3], [4, 5], [6, 7]],
            ins=[cc_in.opt()], outs=[cc_out.opt()]).then_inc(scc.h, 1)
        for g_i in range(NT):
            for h in range(16):
                hg = g_i * 16 + h
                kvh = h // 4
                if h % 4 == 0:
                    kvidx = g_i * 4 + kvh
                    if kvidx >= 2:
                        g.wait_ge(sza.h, (kvidx - 1) * 4)
                    g.wait_ge(scc.h, 1)
                    for r in range(2):
                        g.dma_start(out=kbuf[kvidx % 2][:, r * 2048:(r + 1) * 2048],
                                    in_=cc_out[r * 128:(r + 1) * 128,
                                               kvh * 2048:(kvh + 1) * 2048]
                                    ).then_inc(skl.h, 16)
                        g.dma_start(out=vbuf[kvidx % 2][:, r * 2048:(r + 1) * 2048],
                                    in_=cc_out[r * 128:(r + 1) * 128,
                                               8192 + kvh * 2048:8192 + (kvh + 1) * 2048]
                                    ).then_inc(skl.h, 16)
                if hg >= 2:
                    g.wait_ge(sza.h, hg - 1)
                g.dma_start(out=qh[hg % 2][:], in_=q_d[h][:, g_i * TC:(g_i + 1) * TC]
                            ).then_inc(sql.h, 16)
            for dout in range(ND):
                gd = g_i * ND + dout
                if gd >= 2:
                    g.wait_ge(smo.h, gd - 1)
                g.dma_start(out=woblk[gd % 2][:], in_=wo[dout]).then_inc(swo.h, 16)

    def f_tensor(t):
        t.wait_ge(com.init.h, Common.READY)
        for c in range(NT):
            t.wait_ge(sx.h, 256 * (c + 1))
            ln.tensor_chunk(t, c, x_sb, xcol, PS[0], PS[1])
        t.wait_ge(swi.h, 16)
        for c in range(NT):
            t.wait_ge(sxnl.h, 16 * (c + 1))
            for gi in range(NGRP):
                i = c * NGRP + gi
                if i >= 2:
                    es, ev = evt[i - 2]
                    t.wait_ge(es.h, ev)
                pg = PS[2 + i % 2]
                if gi < 4:
                    t.wait_ge(swp.h, 16 * (c * 20 + gi + 1))
                    for k in range(ND):
                        mm = nc.tensor.matmul(pg[:], wblk[i % 2][:, k * 128:(k + 1) * 128],
                                              xnc[c % 2][:, xcol(k)],
                                              start=(k == 0), stop=(k == ND - 1))
                elif gi < 8:
                    j = gi - 4
                    for k in range(ND):
                        mm = nc.tensor.matmul(pg[:],
                                              xnc[c % 2][:, k * TC + j * 128:k * TC + (j + 1) * 128],
                                              wv_sb[:, k * 512:(k + 1) * 512],
                                              start=(k == 0), stop=(k == ND - 1))
                else:
                    h = gi - 8
                    t.wait_ge(swp.h, 16 * (c * 20 + 4 + h + 1))
                    for k in range(ND):
                        mm = nc.tensor.matmul(pg[:], wblk[i % 2][:, k * 128:(k + 1) * 128],
                                              xnc[c % 2][:, xcol(k)],
                                              start=(k == 0), stop=(k == ND - 1))
                mm.then_inc(smb.h, 1)
        for g in range(NT):
            for h in range(16):
                hg = g * 16 + h
                kvh = h // 4
                kvidx = g * 4 + kvh
                t.wait_ge(sob.h, g * 16)
                t.wait_ge(sql.h, 16 * (hg + 1))
                t.wait_ge(skl.h, 64 * (kvidx + 1))
                for kc in range(32):
                    if 32 * hg + kc >= 2:
                        t.wait_ge(sxp.h, 32 * hg + kc - 1)
                    mm = nc.tensor.matmul(PS[2 + kc % 2][:],
                                          kbuf[kvidx % 2][:, kc * 128:(kc + 1) * 128],
                                          qh[hg % 2][:],
                                          start=True, stop=True)
                    mm.then_inc(sms.h, 1)
                for i in range(4):
                    t.wait_ge(smi.h, max(0, 12 * hg + 3 * i - 1))
                    mm = nc.tensor.matmul(PS[4][:], qh[hg % 2][:, i * 128:(i + 1) * 128],
                                          k_own[:, kvh * TOK + g * TC:kvh * TOK + (g + 1) * TC],
                                          start=True, stop=True)
                    mm.then_inc(smi.h, 1)
                    t.wait_ge(smi.h, 12 * hg + 3 * i + 3)
                    for jj in range(4):
                        mm = nc.tensor.transpose(PSB[:, jj * 128:(jj + 1) * 128],
                                                 pq[:, jj * 128:(jj + 1) * 128],
                                                 com.ident_b[:])
                        mm.then_inc(stp.h, 1)
                t.wait_ge(sxp.h, 32 * (hg + 1))
                t.wait_ge(smi.h, 12 * (hg + 1))
                t.wait_ge(stv.h, 16 * (hg + 1))
                t.wait_ge(soz.h, 5 * hg)
                for ch in range(36):
                    if ch < 32:
                        vsrc = vbuf[kvidx % 2][:, ch * 128:(ch + 1) * 128]
                    else:
                        j = ch - 32
                        vsrc = v_own[:, (g * 4 + j) * 512 + kvh * 128:
                                     (g * 4 + j) * 512 + (kvh + 1) * 128]
                    nc.tensor.matmul(PS[5][0:1, :], com.ones_col_b[:],
                                     pT[:, ch * 512:(ch + 1) * 512],
                                     start=(ch == 0), stop=(ch == 35))
                    mm = nc.tensor.matmul(PS[6][:], vsrc,
                                          pT[:, ch * 512:(ch + 1) * 512],
                                          start=(ch == 0), stop=(ch == 35))
                mm.then_inc(sza.h, 1)
                t.wait_ge(soz.h, 5 * hg + 2)
                nc.tensor.matmul(PS[5][:], com.ones_row[:], zrow[0:1, :],
                                 start=True, stop=True).then_inc(soz.h, 1)
            t.wait_ge(soz.h, 80 * (g + 1))
            for dout in range(ND):
                gd = g * ND + dout
                t.wait_ge(swo.h, 16 * (gd + 1))
                if gd >= 2:
                    t.wait_ge(sob.h, gd - 1)
                pd = PS[2 + gd % 2]
                for k in range(ND):
                    mm = nc.tensor.matmul(pd[:], woblk[gd % 2][:, k * 128:(k + 1) * 128],
                                          o_all[:, k * 512:(k + 1) * 512],
                                          start=(k == 0), stop=(k == ND - 1))
                mm.then_inc(smo.h, 1)

    def f_vector(v):
        v.wait_ge(com.init.h, Common.READY)
        v.wait_ge(srt.h, 32)
        for c in range(NT):
            ln.vector_chunk(v, c, PS[0], PS[1])
            v.wait_ge(ln.s_bc.h, c + 1)
            if c >= 1:
                v.wait_ge(sxnst.h, 256 * c)
            emit_normalize(nc, v, snrm, snrm2, c * ND, x_sb, xcol, xn_sb,
                           xcol, PS[0], PS[1], sub3=sub3)
        for c in range(NT):
            for gi in list(range(4)) + list(range(8, NGRP)):
                i = c * NGRP + gi
                b = rp_base(c, gi)
                isq = gi >= 8
                cs, sn = (rcq, rsq) if isq else (rck, rsk)
                if isq:
                    h = gi - 8
                    if c * 16 + h >= 2:
                        v.wait_ge(sqst.h, 16 * (c * 16 + h - 1))
                    dst = qbf[(c * 16 + h) % 2][:]
                else:
                    kvh = gi
                    dst = k_own[:, kvh * TOK + c * TC:kvh * TOK + (c + 1) * TC]
                pg = PS[2 + i % 2]
                v.wait_ge(smb.h, i + 1)
                v.wait_ge(srp.h, b)
                nc.vector.tensor_tensor(rcs[:], pg[:], cs[:], ALU.mult).then_inc(srp.h, 1)
                nc.vector.tensor_copy(rtmp[0:64, :], pg[64:128, :]).then_inc(srp.h, 1)
                nc.vector.tensor_copy(rtmp[64:128, :], pg[0:64, :]).then_inc(srp.h, 1)
                v.wait_ge(srp.h, b + 3)
                nc.vector.tensor_tensor(rtmp[:], rtmp[:], sn[:], ALU.mult).then_inc(srp.h, 1)
                v.wait_ge(srp.h, b + 4)
                nc.vector.tensor_tensor(dst, rcs[:], rtmp[:], ALU.add).then_inc(srp.h, 1)
        for g in range(NT):
            for h in range(16):
                hg = g * 16 + h
                for i in range(4):
                    v.wait_ge(smi.h, 12 * hg + 3 * i + 1)
                    nc.vector.tensor_tensor(intra_s[:], PS[4][:],
                                            trimask[:, i * 512:(i + 1) * 512],
                                            ALU.add).then_inc(smi.h, 1)
                    v.wait_ge(stp.h, 16 * hg + 4 * i + 4)
                    for jj in range(4):
                        nc.vector.tensor_copy(
                            pT[:, (32 + jj) * 512 + i * 128:(32 + jj) * 512 + (i + 1) * 128],
                            PSB[:, jj * 128:(jj + 1) * 128]).then_inc(stv.h, 1)
                v.wait_ge(sza.h, hg + 1)
                nc.vector.tensor_copy(zrow[0:1, :], PS[5][0:1, :]).then_inc(soz.h, 1)
                v.wait_ge(soz.h, 5 * hg + 1)
                nc.vector.reciprocal(zrow[0:1, :], zrow[0:1, :]).then_inc(soz.h, 1)
                v.wait_ge(soz.h, 5 * hg + 3)
                nc.vector.tensor_copy(of[:], PS[6][:]).then_inc(soz.h, 1)
                v.wait_ge(soz.h, 5 * hg + 4)
                v.wait_ge(smo.h, g * 16)
                nc.vector.tensor_tensor(o_all[:, h * 512:(h + 1) * 512], of[:],
                                        PS[5][:], ALU.mult).then_inc(soz.h, 1)
            for dout in range(ND):
                gd = g * ND + dout
                v.wait_ge(smo.h, gd + 1)
                v.wait_ge(sxr.h, 16 * (gd + 1))
                if gd >= 2:
                    v.wait_ge(sst.h, 16 * (gd - 1))
                nc.vector.tensor_add(ob[gd % 2][:], xr[gd % 2][:],
                                     PS[2 + gd % 2][:]).then_inc(sob.h, 1)

    return dict(sync=f_sync, gpsimd=f_gpsimd, tensor=f_tensor,
                scalar=f_scalar, vector=f_vector,
                done=(sst, 16 * ND * NT))



MLA_SCALE = float(192.0 ** -0.5)
MLA_EPS = 1e-6


def emit_attn_mla(p: Prog, com: Common, tag: str, x_in, x_out, wq, wkva,
                  wkvb_k, wkvb_v, wo, gamma_d, cossin_d, maskv_d, trimask_d,
                  prev=None):
    nc = p.nc
    xn_d = [p.dram(f"{tag}xnd{c}", [128, ND * TC], BF16) for c in range(NT)]
    qn_d = p.dram(tag + "qnd", [16, 128, TOK], BF16)
    qp_d = p.dram(tag + "qpd", [16, 64, TOK], BF16)
    cc_in = p.dram(tag + "ccin", [128, 5 * 2048], BF16)
    cc_out = p.dram(tag + "ccout", [256, 5 * 2048], BF16)
    o_d = [p.dram(f"{tag}od{g}", [128, ND * 512], BF16) for g in range(NT)]

    ln = LNCore(p, com, tag + "L")
    cn_own = p.sbuf([128, 4 * TOK], BF16, tag + "cno")
    kpe_own = p.sbuf([128, TOK], BF16, tag + "kpo")
    gamma = p.sbuf([128, 4], F32, tag + "gam")
    cs_sb = p.sbuf([1, 128], F32, tag + "csr")
    rcol = p.sbuf([64, 6], F32, tag + "rc")
    maskv = p.sbuf([128, NT * 32], F32, tag + "mkv")
    trimask = p.sbuf([128, 4 * TC], F32, tag + "tri")
    qbf = [p.sbuf([128, TC], BF16, f"{tag}qb{i}") for i in range(2)]
    kpe_all = p.sbuf([64, SEQ], BF16, tag + "kpa")
    A = Arena(p, tag + "ar", 35840)
    x_sb = A.f32(ND * TC)
    xn_sb = A.bf16(ND * TC // 2)
    sub3 = A.f32(3 * TC)
    xnc = [A.bf16(ND * TC // 2) for i in range(2)]
    ckv_f = A.f32(4 * TC)
    sqs = A.f32(TC)
    rtmp = A.f32(TC)
    rcs = A.f32(TC)
    wblk = [A.bf16(D // 2) for i in range(2)]
    rowsq = A.f32(6 * TC)
    A.reset(0)
    cn_all = A.bf16(4 * SEQ // 2)
    k_h = A.bf16(SEQ // 2)
    v_h = A.bf16(SEQ // 2)
    k_own_h = A.bf16(TOK // 2)
    v_own_h = A.bf16(TOK // 2)
    pT = A.bf16(36 * 512 // 2)
    qnh = [A.bf16(256) for i in range(2)]
    qph = [A.bf16(256) for i in range(2)]
    pq = A.bf16(256)
    intra_s = A.f32(512)
    of = A.f32(512)
    zrow = A.f32(512)
    osl = A.bf16(256)
    o_img = A.bf16(ND * 512 // 2)
    woblk = [A.bf16(D // 2) for i in range(2)]
    xr = [A.f32(512) for i in range(2)]
    ob = [A.f32(512) for i in range(2)]
    wkb = [A.bf16(256) for i in range(2)]
    wvb2 = [A.bf16(256) for i in range(2)]
    PS = [p.psum([128, 512], F32, f"{tag}ps{i}") for i in range(7)]
    PSB = p.psum([128, 1024], BF16, tag + "psb")

    sx = p.sem(tag + "sx")
    sxnst = p.sem(tag + "sxnst")
    sxnl = p.sem(tag + "sxnl")
    snrm = p.sem(tag + "snrA")
    snrm2 = p.sem(tag + "snrB")
    srt = p.sem(tag + "srt")
    srt2 = p.sem(tag + "srt2")
    srr = p.sem(tag + "srr")
    swp = p.sem(tag + "swp")
    smb = p.sem(tag + "smb")
    srp = p.sem(tag + "srp")
    sqq = p.sem(tag + "sqq")
    smsq = p.sem(tag + "smsq")
    sbcr = p.sem(tag + "sbcr")
    sqst = p.sem(tag + "sqst")
    skst = p.sem(tag + "skst")
    scc = p.sem(tag + "scc")
    scl = p.sem(tag + "scl")
    skv = p.sem(tag + "skv")
    skve = p.sem(tag + "skve")
    sql = p.sem(tag + "sql")
    sms = p.sem(tag + "sms")
    sxp = p.sem(tag + "sxp")
    smi = p.sem(tag + "smi")
    stp = p.sem(tag + "stp")
    stv = p.sem(tag + "stv")
    sza = p.sem(tag + "sza")
    soz = p.sem(tag + "soz")
    sost = p.sem(tag + "sost")
    sol = p.sem(tag + "sol")
    swo = p.sem(tag + "swo")
    skw = p.sem(tag + "skw")
    sxr = p.sem(tag + "sxr")
    smo = p.sem(tag + "smo")
    sob = p.sem(tag + "sob")
    sst = p.sem(tag + "sst")

    xcol = lambda k: slice(k * TC, (k + 1) * TC)
    NGRP = 37
    rp = {}
    cnt = 0
    for c in range(NT):
        for gi in range(NGRP):
            i = c * NGRP + gi
            if gi < 4:
                rp[i] = (cnt, cnt + 1); cnt += 1
            elif gi == 4:
                rp[i] = (cnt, cnt + 5); cnt += 5
                rp[(c, "rms")] = (cnt, cnt + 17); cnt += 17
            elif gi < 21:
                rp[i] = (cnt, cnt + 1); cnt += 1
            else:
                rp[i] = (cnt, cnt + 5); cnt += 5
    RPT = cnt

    evt = {}
    for c in range(NT):
        for gi in range(NGRP):
            i = c * NGRP + gi
            if gi < 4:
                evt[i] = (srp, rp[i][1])
            elif gi == 4:
                evt[i] = (srp, rp[i][0] + 3)
            elif gi < 21:
                evt[i] = (srp, rp[i][1])
            else:
                evt[i] = (srp, rp[i][0] + 3)

    def f_sync(sync):
        if prev is not None:
            sync.wait_ge(prev[0].h, prev[1])
        sync.dma_start(out=cs_sb[:], in_=cossin_d).then_inc(srt2.h, 16)
        sync.dma_start(out=gamma[:], in_=gamma_d).then_inc(srt2.h, 16)
        for c in range(NT):
            for k in range(ND):
                sync.dma_start(out=x_sb[:, xcol(k)],
                               in_=x_in[k * 128:(k + 1) * 128, c * TC:(c + 1) * TC]
                               ).then_inc(sx.h, 16)
            sync.wait_ge(snrm2.h, ND * (c + 1))
            for k in range(ND):
                sync.dma_start(out=xn_d[c][:, xcol(k)], in_=xn_sb[:, xcol(k)]
                               ).then_inc(sxnst.h, 16)
        for c in range(NT):
            sync.wait_ge(srp.h, rp[c * NGRP + 4][1])
            sync.dma_start(out=cc_in[:, 4 * 2048 + c * TC:4 * 2048 + (c + 1) * TC],
                           in_=kpe_own[:, c * TC:(c + 1) * TC]).then_inc(skst.h, 16)
            for h in range(16):
                i = c * NGRP + 5 + h
                sync.wait_ge(srp.h, rp[i][1])
                sync.dma_start(out=qn_d[h][:, c * TC:(c + 1) * TC],
                               in_=qbf[i % 2][:]).then_inc(sqst.h, 16)
            for h in range(16):
                i = c * NGRP + 21 + h
                sync.wait_ge(srp.h, rp[i][1])
                sync.dma_start(out=qp_d[h][:, c * TC:(c + 1) * TC],
                               in_=qbf[i % 2][0:64, :]).then_inc(sqst.h, 16)
            sync.wait_ge(srp.h, rp[(c, "rms")][1])
            for kc in range(4):
                sync.dma_start(
                    out=cc_in[:, kc * 2048 + c * TC:kc * 2048 + (c + 1) * TC],
                    in_=cn_own[:, kc * TOK + c * TC:kc * TOK + (c + 1) * TC]
                    ).then_inc(skst.h, 16)
        for g in range(NT):
            sync.wait_ge(sost.h, 16 * (61 + g))
            if g >= 1:
                sync.wait_ge(smo.h, g * ND)
            sync.dma_start(out=o_img[:], in_=o_d[g]).then_inc(sol.h, 16)
            for dout in range(ND):
                gd = g * ND + dout
                if gd >= 2:
                    sync.wait_ge(sob.h, gd - 1)
                sync.dma_start(out=xr[gd % 2][:],
                               in_=x_in[dout * 128:(dout + 1) * 128,
                                        g * TC:(g + 1) * TC]).then_inc(sxr.h, 16)
                sync.wait_ge(sob.h, gd + 1)
                sync.dma_start(out=x_out[dout * 128:(dout + 1) * 128,
                                         g * TC:(g + 1) * TC],
                               in_=ob[gd % 2][:]).then_inc(sst.h, 16)
        sync.wait_ge(sst.h, 16 * ND * NT)

    def f_gpsimd(g):
        g.dma_start(out=maskv[:], in_=maskv_d).then_inc(srt.h, 16)
        g.dma_start(out=trimask[:], in_=trimask_d).then_inc(srt.h, 16)
        for c in range(NT):
            for gi in range(NGRP):
                i = c * NGRP + gi
                if i >= 2:
                    g.wait_ge(smb.h, i - 1)
                if gi < 5:
                    src = wkva[gi]
                elif gi < 21:
                    src = wq[2 * (gi - 5)]
                else:
                    src = wq[2 * (gi - 21) + 1]
                g.dma_start(out=wblk[i % 2][:], in_=src).then_inc(swp.h, 16)
        g.wait_ge(skst.h, 16 * 5 * NT)
        nc.gpsimd.collective_compute(
            "AllGather", ALU.bypass,
            replica_groups=[[0, 1], [2, 3], [4, 5], [6, 7]],
            ins=[cc_in.opt()], outs=[cc_out.opt()]).then_inc(scc.h, 1)
        g.wait_ge(scc.h, 1)
        for r in range(2):
            for kc in range(4):
                g.dma_start(out=cn_all[:, kc * SEQ + r * 2048:kc * SEQ + (r + 1) * 2048],
                            in_=cc_out[r * 128:(r + 1) * 128,
                                       kc * 2048:(kc + 1) * 2048]).then_inc(scl.h, 16)
            g.dma_start(out=kpe_all[0:64, r * 2048:(r + 1) * 2048],
                        in_=cc_out[r * 128:r * 128 + 64,
                                   4 * 2048:5 * 2048]).then_inc(scl.h, 16)
        for h in range(16):
            if h >= 2:
                g.wait_ge(skv.h, (h - 1) * 60)
            g.dma_start(out=wkb[h % 2][:], in_=wkvb_k[h]).then_inc(skw.h, 16)
            g.dma_start(out=wvb2[h % 2][:], in_=wkvb_v[h]).then_inc(skw.h, 16)
            for g_i in range(NT):
                hg = h * NT + g_i
                if hg >= 2:
                    g.wait_ge(sza.h, hg - 1)
                g.dma_start(out=qnh[hg % 2][:], in_=qn_d[h][:, g_i * TC:(g_i + 1) * TC]
                            ).then_inc(sql.h, 16)
                g.dma_start(out=qph[hg % 2][0:64, :], in_=qp_d[h][:, g_i * TC:(g_i + 1) * TC]
                            ).then_inc(sql.h, 16)
        for g_i in range(NT):
            for dout in range(ND):
                gd = g_i * ND + dout
                if gd >= 2:
                    g.wait_ge(smo.h, gd - 1)
                g.dma_start(out=woblk[gd % 2][:], in_=wo[dout]).then_inc(swo.h, 16)

    def f_tensor(t):
        t.wait_ge(com.init.h, Common.READY)
        t.wait_ge(srt2.h, 16)
        t.wait_ge(com.init.h, Common.READY)
        mm = nc.tensor.matmul(PS[6][0:128, 0:1], cs_sb[:], com.ones_row[0:1, 0:1],
                              start=True, stop=True)
        mm.then_inc(srr.h, 1)
        for c in range(NT):
            t.wait_ge(sx.h, 256 * (c + 1))
            ln.tensor_chunk(t, c, x_sb, xcol, PS[0], PS[1])
        for c in range(NT):
            t.wait_ge(sxnl.h, 16 * (c + 1))
            t.wait_ge(srr.h, 10)
            for gi in range(NGRP):
                i = c * NGRP + gi
                t.wait_ge(swp.h, 16 * (i + 1))
                if i >= 2:
                    es, ev = evt[i - 2]
                    t.wait_ge(es.h, ev)
                pg = PS[2 + i % 2]
                for k in range(ND):
                    mm = nc.tensor.matmul(pg[:], wblk[i % 2][:, k * 128:(k + 1) * 128],
                                          xnc[c % 2][:, xcol(k)],
                                          start=(k == 0), stop=(k == ND - 1))
                mm.then_inc(smb.h, 1)
                if gi == 4:
                    for kc in range(4):
                        t.wait_ge(sqq.h, 4 * c + kc + 1)
                        nc.tensor.matmul(PS[4][0:1, :], com.ones_col[:], sqs[:],
                                         start=(kc == 0), stop=(kc == 3)
                                         ).then_inc(smsq.h, 1)
                    t.wait_ge(srp.h, rp[(c, "rms")][0] + 9)
                    nc.tensor.matmul(PS[5][:], com.ones_row[:],
                                     rowsq[0:1, 5 * TC:6 * TC],
                                     start=True, stop=True).then_inc(sbcr.h, 1)
        for h in range(16):
            t.wait_ge(scl.h, 16 * 10)
            t.wait_ge(skw.h, 32 * (h + 1))
            t.wait_ge(sxp.h, 128 * h)
            base = h * 60
            for tt in range(8):
                kg = base + tt
                if kg >= 2:
                    t.wait_ge(skve.h, kg - 1)
                pg = PS[2 + kg % 2]
                for kc in range(4):
                    mm = nc.tensor.matmul(pg[:], wkb[h % 2][:, kc * 128:(kc + 1) * 128],
                                          cn_all[:, kc * SEQ + tt * 512:kc * SEQ + (tt + 1) * 512],
                                          start=(kc == 0), stop=(kc == 3))
                mm.then_inc(skv.h, 1)
            for tt in range(32):
                kg = base + 8 + tt
                t.wait_ge(skve.h, kg - 1)
                pg = PS[2 + kg % 2]
                for kc in range(4):
                    mm = nc.tensor.matmul(pg[:, 0:128],
                                          cn_all[:, kc * SEQ + tt * 128:kc * SEQ + (tt + 1) * 128],
                                          wvb2[h % 2][:, kc * 128:(kc + 1) * 128],
                                          start=(kc == 0), stop=(kc == 3))
                mm.then_inc(skv.h, 1)
            for tt in range(4):
                kg = base + 40 + tt
                t.wait_ge(skve.h, kg - 1)
                pg = PS[2 + kg % 2]
                for kc in range(4):
                    mm = nc.tensor.matmul(pg[:], wkb[h % 2][:, kc * 128:(kc + 1) * 128],
                                          cn_own[:, kc * TOK + tt * 512:kc * TOK + (tt + 1) * 512],
                                          start=(kc == 0), stop=(kc == 3))
                mm.then_inc(skv.h, 1)
            for tt in range(16):
                kg = base + 44 + tt
                t.wait_ge(skve.h, kg - 1)
                pg = PS[2 + kg % 2]
                for kc in range(4):
                    mm = nc.tensor.matmul(pg[:, 0:128],
                                          cn_own[:, kc * TOK + tt * 128:kc * TOK + (tt + 1) * 128],
                                          wvb2[h % 2][:, kc * 128:(kc + 1) * 128],
                                          start=(kc == 0), stop=(kc == 3))
                mm.then_inc(skv.h, 1)
            for g_i in range(NT):
                hg = h * NT + g_i
                t.wait_ge(skve.h, base + 60)
                t.wait_ge(sql.h, 32 * (hg + 1))
                for kc in range(32):
                    if 32 * hg + kc >= 2:
                        t.wait_ge(sxp.h, 32 * hg + kc - 1)
                    nc.tensor.matmul(PS[2 + kc % 2][:],
                                     k_h[:, kc * 128:(kc + 1) * 128],
                                     qnh[hg % 2][:], start=True, stop=False)
                    mm = nc.tensor.matmul(PS[2 + kc % 2][:],
                                          kpe_all[0:64, kc * 128:(kc + 1) * 128],
                                          qph[hg % 2][0:64, :], start=False, stop=True)
                    mm.then_inc(sms.h, 1)
                for i in range(4):
                    t.wait_ge(smi.h, max(0, 12 * hg + 3 * i - 1))
                    nc.tensor.matmul(PS[4][:], qnh[hg % 2][:, i * 128:(i + 1) * 128],
                                     k_own_h[:, g_i * TC:(g_i + 1) * TC],
                                     start=True, stop=False)
                    mm = nc.tensor.matmul(PS[4][:], qph[hg % 2][0:64, i * 128:(i + 1) * 128],
                                          kpe_own[0:64, g_i * TC:(g_i + 1) * TC],
                                          start=False, stop=True)
                    mm.then_inc(smi.h, 1)
                    t.wait_ge(smi.h, 12 * hg + 3 * i + 3)
                    for jj in range(4):
                        mm = nc.tensor.transpose(PSB[:, jj * 128:(jj + 1) * 128],
                                                 pq[:, jj * 128:(jj + 1) * 128],
                                                 com.ident_b[:])
                        mm.then_inc(stp.h, 1)
                t.wait_ge(sxp.h, 32 * (hg + 1))
                t.wait_ge(smi.h, 12 * (hg + 1))
                t.wait_ge(stv.h, 16 * (hg + 1))
                t.wait_ge(soz.h, 5 * hg)
                for ch in range(36):
                    if ch < 32:
                        vsrc = v_h[:, ch * 128:(ch + 1) * 128]
                    else:
                        j = ch - 32
                        vsrc = v_own_h[:, (g_i * 4 + j) * 128:(g_i * 4 + j + 1) * 128]
                    nc.tensor.matmul(PS[5][0:1, :], com.ones_col_b[:],
                                     pT[:, ch * 512:(ch + 1) * 512],
                                     start=(ch == 0), stop=(ch == 35))
                    mm = nc.tensor.matmul(PS[6][:], vsrc,
                                          pT[:, ch * 512:(ch + 1) * 512],
                                          start=(ch == 0), stop=(ch == 35))
                mm.then_inc(sza.h, 1)
                t.wait_ge(soz.h, 5 * hg + 2)
                nc.tensor.matmul(PS[5][:], com.ones_row[:], zrow[0:1, :],
                                 start=True, stop=True).then_inc(soz.h, 1)
        for g_i in range(NT):
            t.wait_ge(sol.h, 16 * (g_i + 1))
            for dout in range(ND):
                gd = g_i * ND + dout
                t.wait_ge(swo.h, 16 * (gd + 1))
                if gd >= 2:
                    t.wait_ge(sob.h, gd - 1)
                pd = PS[2 + gd % 2]
                for k in range(ND):
                    mm = nc.tensor.matmul(pd[:], woblk[gd % 2][:, k * 128:(k + 1) * 128],
                                          o_img[:, k * 512:(k + 1) * 512],
                                          start=(k == 0), stop=(k == ND - 1))
                mm.then_inc(smo.h, 1)

    def f_scalar(s):
        s.wait_ge(com.init.h, Common.READY)
        for c in range(NT):
            s.wait_ge(sx.h, 256 * (c + 1))
            ln.scalar_chunk(s, c, x_sb, xcol)
        for c in range(NT):
            s.wait_ge(sxnst.h, 256 * (c + 1))
            if c >= 2:
                s.wait_ge(smb.h, NGRP * (c - 1))
            s.dma_start(out=xnc[c % 2][:], in_=xn_d[c]).then_inc(sxnl.h, 16)
            for kc in range(4):
                s.wait_ge(srp.h, rp[c * NGRP + kc][1])
                if 4 * c + kc >= 1:
                    s.wait_ge(smsq.h, 4 * c + kc)
                nc.scalar.activation(sqs[:], ckv_f[:, kc * TC:(kc + 1) * TC],
                                     AF.Square).then_inc(sqq.h, 1)
            s.wait_ge(srp.h, rp[(c, "rms")][0] + 3)
            nc.scalar.activation(rowsq[0:1, 2 * TC:3 * TC], rowsq[0:1, TC:2 * TC],
                                 AF.Sqrt).then_inc(srp.h, 1)
        for h in range(16):
            for g_i in range(NT):
                hg = h * NT + g_i
                s.wait_ge(sza.h, hg)
                for kc in range(32):
                    s.wait_ge(sms.h, 32 * hg + kc + 1)
                    nc.scalar.activation(
                        pT[:, kc * 512:(kc + 1) * 512], PS[2 + kc % 2][:],
                        AF.Exp, bias=maskv[:, g_i * 32 + kc:g_i * 32 + kc + 1]
                        ).then_inc(sxp.h, 1)
                for i in range(4):
                    if i >= 1:
                        s.wait_ge(stp.h, 16 * hg + 4 * i)
                    s.wait_ge(smi.h, 12 * hg + 3 * i + 2)
                    nc.scalar.activation(pq[:], intra_s[:], AF.Exp
                                         ).then_inc(smi.h, 1)
                s.wait_ge(soz.h, 5 * hg + 5)
                s.dma_start(out=o_d[g_i][:, h * 512:(h + 1) * 512], in_=osl[:]
                            ).then_inc(sost.h, 16)

    def f_vector(v):
        v.wait_ge(com.init.h, Common.READY)
        v.wait_ge(srt.h, 32)
        v.wait_ge(srt2.h, 32)
        nc.vector.memset(kpe_own[:], 0.0)
        v.wait_ge(srr.h, 1)
        nc.vector.tensor_copy(rcol[:, 0:1], PS[6][0:64, 0:1]).then_inc(srr.h, 1)
        nc.vector.tensor_copy(rcol[:, 1:2], PS[6][64:128, 0:1]).then_inc(srr.h, 1)
        v.wait_ge(srr.h, 3)
        nc.vector.tensor_scalar_mul(rcol[0:32, 1:2], rcol[0:32, 1:2], -1.0
                                    ).then_inc(srr.h, 1)
        v.wait_ge(srr.h, 4)
        nc.vector.tensor_scalar_mul(rcol[:, 2:3], rcol[:, 0:1], MLA_SCALE
                                    ).then_inc(srr.h, 1)
        v.wait_ge(srr.h, 5)
        nc.vector.tensor_scalar_mul(rcol[:, 3:4], rcol[:, 1:2], MLA_SCALE
                                    ).then_inc(srr.h, 5)
        for c in range(NT):
            ln.vector_chunk(v, c, PS[0], PS[1])
            v.wait_ge(ln.s_bc.h, c + 1)
            if c >= 1:
                v.wait_ge(sxnst.h, 256 * c)
            emit_normalize(nc, v, snrm, snrm2, c * ND, x_sb, xcol, xn_sb,
                           xcol, PS[0], PS[1], sub3=sub3)
        def rms_chain(c):
            b2 = rp[(c, "rms")][0]
            v.wait_ge(smsq.h, 4 * (c + 1))
            nc.vector.tensor_scalar(rowsq[0:1, TC:2 * TC], PS[4][0:1, :],
                                    1.0 / 512.0, float(MLA_EPS), ALU.mult, ALU.add
                                    ).then_inc(srp.h, 1)
            v.wait_ge(srp.h, b2 + 1)
            nc.vector.tensor_copy(rowsq[0:1, 0:TC], rowsq[0:1, TC:2 * TC]
                                  ).then_inc(srp.h, 1)
            v.wait_ge(srp.h, b2 + 2)
            nc.vector.tensor_copy(rowsq[0:1, 3 * TC:4 * TC], rowsq[0:1, 0:TC]
                                  ).then_inc(srp.h, 1)
            v.wait_ge(srp.h, b2 + 4)
            nc.vector.reciprocal(rowsq[0:1, 4 * TC:5 * TC], rowsq[0:1, 2 * TC:3 * TC]
                                 ).then_inc(srp.h, 1)
            v.wait_ge(srp.h, b2 + 5)
            nc.vector.tensor_tensor(rowsq[0:1, 3 * TC:4 * TC],
                                    rowsq[0:1, 4 * TC:5 * TC],
                                    rowsq[0:1, 4 * TC:5 * TC], ALU.mult
                                    ).then_inc(srp.h, 1)
            v.wait_ge(srp.h, b2 + 6)
            nc.vector.tensor_tensor(rowsq[0:1, 3 * TC:4 * TC],
                                    rowsq[0:1, 3 * TC:4 * TC],
                                    rowsq[0:1, TC:2 * TC], ALU.mult
                                    ).then_inc(srp.h, 1)
            v.wait_ge(srp.h, b2 + 7)
            nc.vector.tensor_scalar(rowsq[0:1, 3 * TC:4 * TC],
                                    rowsq[0:1, 3 * TC:4 * TC], -0.5, 1.5,
                                    ALU.mult, ALU.add).then_inc(srp.h, 1)
            v.wait_ge(srp.h, b2 + 8)
            nc.vector.tensor_tensor(rowsq[0:1, 5 * TC:6 * TC],
                                    rowsq[0:1, 4 * TC:5 * TC],
                                    rowsq[0:1, 3 * TC:4 * TC], ALU.mult
                                    ).then_inc(srp.h, 1)
            v.wait_ge(sbcr.h, c + 1)
            for kc in range(4):
                st = b2 + 9 + 2 * kc
                v.wait_ge(srp.h, st)
                nc.vector.tensor_tensor(rtmp[:], ckv_f[:, kc * TC:(kc + 1) * TC],
                                        PS[5][:], ALU.mult).then_inc(srp.h, 1)
                v.wait_ge(srp.h, st + 1)
                nc.vector.tensor_scalar_mul(
                    cn_own[:, kc * TOK + c * TC:kc * TOK + (c + 1) * TC],
                    rtmp[:], gamma[:, kc:kc + 1]).then_inc(srp.h, 1)

        for c in range(NT):
            for gi in range(NGRP):
                i = c * NGRP + gi
                b = rp[i][0]
                pg = PS[2 + i % 2]
                v.wait_ge(smb.h, i + 1)
                if gi < 4:
                    nc.vector.tensor_copy(ckv_f[:, gi * TC:(gi + 1) * TC], pg[:]
                                          ).then_inc(srp.h, 1)
                elif gi == 4 or gi >= 21:
                    scaled = gi >= 21
                    cosc = rcol[:, 2:3] if scaled else rcol[:, 0:1]
                    sinc = rcol[:, 3:4] if scaled else rcol[:, 1:2]
                    if gi == 4:
                        dst = kpe_own[0:64, c * TC:(c + 1) * TC]
                    else:
                        h = gi - 21
                        if (c * 32 + 16 + h) >= 2:
                            v.wait_ge(sqst.h, 16 * (c * 32 + 16 + h - 1))
                        dst = qbf[i % 2][0:64, :]
                    v.wait_ge(srp.h, b)
                    nc.vector.tensor_scalar_mul(rcs[0:64, :], pg[0:64, :], cosc
                                                ).then_inc(srp.h, 1)
                    nc.vector.tensor_copy(rtmp[0:32, :], pg[32:64, :]).then_inc(srp.h, 1)
                    nc.vector.tensor_copy(rtmp[32:64, :], pg[0:32, :]).then_inc(srp.h, 1)
                    v.wait_ge(srp.h, b + 3)
                    nc.vector.tensor_scalar_mul(rtmp[0:64, :], rtmp[0:64, :], sinc
                                                ).then_inc(srp.h, 1)
                    v.wait_ge(srp.h, b + 4)
                    nc.vector.tensor_add(dst, rcs[0:64, :], rtmp[0:64, :]
                                         ).then_inc(srp.h, 1)
                    if gi == 4:
                        rms_chain(c)
                else:
                    h = gi - 5
                    if (c * 32 + h) >= 2:
                        v.wait_ge(sqst.h, 16 * (c * 32 + h - 1))
                    nc.vector.tensor_scalar_mul(qbf[i % 2][:], pg[:], MLA_SCALE
                                                ).then_inc(srp.h, 1)
        for h in range(16):
            base = h * 60
            for kg0 in range(60):
                kg = base + kg0
                v.wait_ge(skv.h, kg + 1)
                pg = PS[2 + kg % 2]
                if kg0 < 8:
                    dsl = k_h[:, kg0 * 512:(kg0 + 1) * 512]
                    nc.vector.tensor_copy(dsl, pg[:]).then_inc(skve.h, 1)
                elif kg0 < 40:
                    tt = kg0 - 8
                    nc.vector.tensor_copy(v_h[:, tt * 128:(tt + 1) * 128],
                                          pg[:, 0:128]).then_inc(skve.h, 1)
                elif kg0 < 44:
                    tt = kg0 - 40
                    nc.vector.tensor_copy(k_own_h[:, tt * 512:(tt + 1) * 512],
                                          pg[:]).then_inc(skve.h, 1)
                else:
                    tt = kg0 - 44
                    nc.vector.tensor_copy(v_own_h[:, tt * 128:(tt + 1) * 128],
                                          pg[:, 0:128]).then_inc(skve.h, 1)
            for g_i in range(NT):
                hg = h * NT + g_i
                for i in range(4):
                    v.wait_ge(smi.h, 12 * hg + 3 * i + 1)
                    nc.vector.tensor_tensor(intra_s[:], PS[4][:],
                                            trimask[:, i * 512:(i + 1) * 512],
                                            ALU.add).then_inc(smi.h, 1)
                    v.wait_ge(stp.h, 16 * hg + 4 * i + 4)
                    for jj in range(4):
                        nc.vector.tensor_copy(
                            pT[:, (32 + jj) * 512 + i * 128:(32 + jj) * 512 + (i + 1) * 128],
                            PSB[:, jj * 128:(jj + 1) * 128]).then_inc(stv.h, 1)
                v.wait_ge(sza.h, hg + 1)
                nc.vector.tensor_copy(zrow[0:1, :], PS[5][0:1, :]).then_inc(soz.h, 1)
                v.wait_ge(soz.h, 5 * hg + 1)
                nc.vector.reciprocal(zrow[0:1, :], zrow[0:1, :]).then_inc(soz.h, 1)
                v.wait_ge(soz.h, 5 * hg + 3)
                nc.vector.tensor_copy(of[:], PS[6][:]).then_inc(soz.h, 1)
                v.wait_ge(soz.h, 5 * hg + 4)
                if hg >= 1:
                    v.wait_ge(sost.h, 16 * hg)
                nc.vector.tensor_tensor(osl[:], of[:], PS[5][:], ALU.mult
                                        ).then_inc(soz.h, 1)
        for g_i in range(NT):
            for dout in range(ND):
                gd = g_i * ND + dout
                v.wait_ge(smo.h, gd + 1)
                v.wait_ge(sxr.h, 16 * (gd + 1))
                if gd >= 2:
                    v.wait_ge(sst.h, 16 * (gd - 1))
                nc.vector.tensor_add(ob[gd % 2][:], xr[gd % 2][:],
                                     PS[2 + gd % 2][:]).then_inc(sob.h, 1)

    return dict(sync=f_sync, gpsimd=f_gpsimd, tensor=f_tensor,
                scalar=f_scalar, vector=f_vector,
                done=(sst, 16 * ND * NT))



WNH = 32
WNB = 16


def emit_attn_win(p: Prog, com: Common, tag: str, x_in, x_out, wqkv, wo,
                  bias_d, ropes_d, winmask_d, sinksk_d, sinksv_d, ssel_d,
                  prev=None, cut=None):
    nc = p.nc
    xn_d = [p.dram(f"{tag}xnd{c}", [128, ND * TC], BF16) for c in range(NT)]
    q_d = p.dram(tag + "qd", [WNB, 128, TOK], BF16)
    ke_d = p.dram(tag + "ked", [WNB, 128, 2304], BF16)
    ve_d = p.dram(tag + "ved", [WNB, 128, 2304], BF16)
    o_d = p.dram(tag + "od", [WNB, 128, TOK], BF16)
    cc_in = p.dram(tag + "ccin", [128, 8192], BF16)
    cc_out = p.dram(tag + "ccout", [256, 8192], BF16)

    ln = LNCore(p, com, tag + "L")
    biases = p.sbuf([128, 48], F32, tag + "bi")
    bo_col = p.sbuf([128, ND], F32, tag + "bo")
    rq = p.sbuf([64, 4 * TC], F32, tag + "rq")
    sinks_k = p.sbuf([128, 256], BF16, tag + "snk")
    sinks_v = p.sbuf([16, 2048], BF16, tag + "snv")
    winmask = p.sbuf([128, 1200], F32, tag + "wm")
    ssel = p.sbuf([128, 4], F32, tag + "ssel")
    qst = [p.sbuf([64, TC], BF16, f"{tag}qs{i}") for i in range(4)]
    kst = [p.sbuf([64, TC], BF16, f"{tag}ks{i}") for i in range(4)]
    vtile = [p.sbuf([128, 128], BF16, f"{tag}vt{i}") for i in range(4)]
    pts = p.sbuf([16, 128], BF16, tag + "pts")
    A = Arena(p, tag + "ar", 27000)
    x_sb = A.f32(ND * TC)
    xn_sb = A.bf16(ND * TC // 2)
    sub3 = A.f32(3 * TC)
    xnc = [A.bf16(ND * TC // 2) for i in range(2)]
    qf = A.f32(2 * TC)
    rx = A.f32(TC)
    ry = A.f32(TC)
    vbf = [A.bf16(TC // 2) for i in range(2)]
    wblk = [A.bf16(D // 2) for i in range(2)]
    A.reset(0)
    gbuf = A.bf16(8192)
    slv = A.bf16(2048)
    A.reset(0)
    ke = [A.bf16(1152) for i in range(2)]
    ve = [A.bf16(1152) for i in range(2)]
    qh = [A.bf16(64) for i in range(2)]
    ss = A.f32(400)
    mxc = A.f32(1)
    zzc = A.f32(2)
    pb = A.f32(400)
    pn = A.bf16(200)
    ptb = A.bf16(192)
    o_blk = A.bf16(TOK // 2)
    o_img = A.bf16(ND * 512 // 2)
    woblk = [A.bf16(D // 2) for i in range(2)]
    xr = [A.f32(512) for i in range(2)]
    ob = [A.f32(512) for i in range(2)]
    PS = [p.psum([128, 512], F32, f"{tag}ps{i}") for i in range(7)]
    PSB = p.psum([128, 1024], BF16, tag + "psb")

    sx = p.sem(tag + "sx")
    sxnst = p.sem(tag + "sxnst")
    sxnl = p.sem(tag + "sxnl")
    snrm = p.sem(tag + "snrA")
    snrm2 = p.sem(tag + "snrB")
    srt = p.sem(tag + "srt")
    srt2 = p.sem(tag + "srt2")
    srl = p.sem(tag + "srl")
    swp = p.sem(tag + "swp")
    smb = p.sem(tag + "smb")
    sev = p.sem(tag + "sev")
    srp = p.sem(tag + "srp")
    stpv = p.sem(tag + "stpv")
    sqst = p.sem(tag + "sqst")
    skst = p.sem(tag + "skst")
    svst = p.sem(tag + "svst")
    scst = p.sem(tag + "scst")
    scc = p.sem(tag + "scc")
    sgl = p.sem(tag + "sgl")
    ssl = p.sem(tag + "ssl")
    sslst = p.sem(tag + "sslst")
    skl = p.sem(tag + "skl")
    sql = p.sem(tag + "sql")
    sms = p.sem(tag + "sms")
    sso = p.sem(tag + "sso")
    stp = p.sem(tag + "stp")
    stv = p.sem(tag + "stv")
    sma = p.sem(tag + "sma")
    sov = p.sem(tag + "sov")
    sost = p.sem(tag + "sost")
    sol = p.sem(tag + "sol")
    swo = p.sem(tag + "swo")
    sxr = p.sem(tag + "sxr")
    smo = p.sem(tag + "smo")
    sob = p.sem(tag + "sob")
    sst = p.sem(tag + "sst")

    novt = cut == "RPa"
    norp = cut == "RPb"
    if cut in ("RPa", "RPb"):
        cut = "RP"
    lvl = {"LN": 0, "MM": 1, "EV": 2, "RP": 3, "A": 4, "B": 5, "E": 6,
           None: 7}[cut]
    xcol = lambda k: slice(k * TC, (k + 1) * TC)
    RPQ, RPV = (0 if norp else 10), (0 if novt else 4)
    rp, evc = {}, {}
    cnt = ecnt = 0
    for c in range(NT):
        for gi in range(48):
            i = c * 48 + gi
            n = RPV if gi >= 32 else RPQ
            rp[i] = (cnt, cnt + n)
            cnt += n
            ecnt += 1 if gi >= 32 else 2
            evc[i] = ecnt
    EVTOT = evc[NT * 48 - 1]
    NSO = 6

    def vgl(i):
        c, gi = divmod(i, 48)
        return c * 16 + (gi - 32)

    def prev_v(i):
        c, gi = divmod(i, 48)
        if gi > 32:
            return i - 1
        return (c - 1) * 48 + 47 if c >= 1 else None

    def prev_qk(i):
        c, gi = divmod(i, 48)
        if 1 <= gi <= 31:
            return i - 1
        return (c - 1) * 48 + 31 if c >= 1 else None

    kcc, vcc = {}, {}
    for b in range(WNB):
        kcc[b] = 16 * 2 * (b + 1)
        kcc[48 + b] = 16 * (48 + 2 * (b + 1))
        vcc[4 * b] = 16 * (32 + b + 1)
        vcc[4 * (48 + b) + 3] = 16 * (80 + b + 1)
    SCTOT = 16 * 96

    def f_sync(sync):
        if prev is not None:
            sync.wait_ge(prev[0].h, prev[1])
        sync.dma_start(out=biases[:], in_=bias_d[:, 0:48]).then_inc(srt2.h, 16)
        sync.dma_start(out=bo_col[:], in_=bias_d[:, 48:64]).then_inc(srt2.h, 16)
        for c in range(NT):
            for k in range(ND):
                sync.dma_start(out=x_sb[:, xcol(k)],
                               in_=x_in[k * 128:(k + 1) * 128, c * TC:(c + 1) * TC]
                               ).then_inc(sx.h, 16)
            sync.wait_ge(snrm2.h, ND * (c + 1))
            for k in range(ND):
                sync.dma_start(out=xn_d[c][:, xcol(k)], in_=xn_sb[:, xcol(k)]
                               ).then_inc(sxnst.h, 16)
        if lvl == 0:
            sync.wait_ge(sxnst.h, 16 * 64)
            return
        if lvl == 1:
            sync.wait_ge(smb.h, 192)
            return
        if lvl == 2:
            sync.wait_ge(sev.h, EVTOT)
            return
        if lvl == 3:
            sync.wait_ge(srp.h, rp[NT * 48 - 1][1])
            return
        for c in range(NT):
            for qb in range(WNB):
                i = c * 48 + qb
                sync.wait_ge(srp.h, rp[i][1])
                sync.dma_start(out=q_d[qb][0:64, c * TC:(c + 1) * TC],
                               in_=qst[2 * (i % 2)][:]).then_inc(sqst.h, 16)
                sync.dma_start(out=q_d[qb][64:128, c * TC:(c + 1) * TC],
                               in_=qst[2 * (i % 2) + 1][:]).then_inc(sqst.h, 16)
            for kb in range(WNB):
                i = c * 48 + 16 + kb
                sync.wait_ge(srp.h, rp[i][1])
                sync.dma_start(out=ke_d[kb][0:64, 128 + c * TC:128 + (c + 1) * TC],
                               in_=kst[2 * (i % 2)][:]).then_inc(skst.h, 16)
                sync.dma_start(out=ke_d[kb][64:128, 128 + c * TC:128 + (c + 1) * TC],
                               in_=kst[2 * (i % 2) + 1][:]).then_inc(skst.h, 16)
                if c == 0:
                    sync.dma_start(out=cc_in[0:64, kb * 128:(kb + 1) * 128],
                                   in_=kst[2 * (i % 2)][:, 0:128]).then_inc(scst.h, 16)
                    sync.dma_start(out=cc_in[64:128, kb * 128:(kb + 1) * 128],
                                   in_=kst[2 * (i % 2) + 1][:, 0:128]).then_inc(scst.h, 16)
                if c == NT - 1:
                    sync.dma_start(out=cc_in[0:64, 2048 + kb * 128:2048 + (kb + 1) * 128],
                                   in_=kst[2 * (i % 2)][:, TC - 128:TC]).then_inc(scst.h, 16)
                    sync.dma_start(out=cc_in[64:128, 2048 + kb * 128:2048 + (kb + 1) * 128],
                                   in_=kst[2 * (i % 2) + 1][:, TC - 128:TC]).then_inc(scst.h, 16)
            for vb in range(WNB):
                i = c * 48 + 32 + vb
                vg = vgl(i)
                for j in range(4):
                    tt = c * 4 + j
                    g_t = 4 * vg + j
                    sync.wait_ge(srp.h, rp[i][0] + j + 1)
                    sync.dma_start(out=ve_d[vb][:, (tt + 1) * 128:(tt + 2) * 128],
                                   in_=vtile[g_t % 4][:]).then_inc(svst.h, 16)
                    if tt == 0:
                        sync.dma_start(out=cc_in[:, 4096 + vb * 128:4096 + (vb + 1) * 128],
                                       in_=vtile[g_t % 4][:]).then_inc(scst.h, 16)
                    if tt == 15:
                        sync.dma_start(out=cc_in[:, 6144 + vb * 128:6144 + (vb + 1) * 128],
                                       in_=vtile[g_t % 4][:]).then_inc(scst.h, 16)
        if lvl == 4:
            sync.wait_ge(sqst.h, 16 * 128)
            sync.wait_ge(skst.h, 16 * 128)
            sync.wait_ge(svst.h, 16 * 256)
            return
        sync.wait_ge(ssl.h, 2)
        for b in range(WNB):
            sync.dma_start(out=ke_d[b][:, 0:128],
                           in_=slv[:, b * 128:(b + 1) * 128]).then_inc(sslst.h, 16)
            sync.dma_start(out=ke_d[b][:, 2176:2304],
                           in_=slv[:, 2048 + b * 128:2048 + (b + 1) * 128]
                           ).then_inc(sslst.h, 16)
        sync.wait_ge(ssl.h, 4)
        for b in range(WNB):
            sync.dma_start(out=ve_d[b][:, 0:128],
                           in_=slv[:, b * 128:(b + 1) * 128]).then_inc(sslst.h, 16)
            sync.dma_start(out=ve_d[b][:, 2176:2304],
                           in_=slv[:, 2048 + b * 128:2048 + (b + 1) * 128]
                           ).then_inc(sslst.h, 16)
        if lvl == 5:
            sync.wait_ge(sslst.h, 16 * 64)
            return
        for c in range(NT):
            if lvl == 6:
                break
            sync.wait_ge(sost.h, 16 * WNB)
            if c >= 1:
                sync.wait_ge(smo.h, c * ND)
            for b in range(WNB):
                sync.dma_start(out=o_img[:, b * 512:(b + 1) * 512],
                               in_=o_d[b][:, c * TC:(c + 1) * TC]).then_inc(sol.h, 16)
            for dout in range(ND):
                gd = c * ND + dout
                if gd >= 2:
                    sync.wait_ge(sob.h, gd - 1)
                sync.dma_start(out=xr[gd % 2][:],
                               in_=x_in[dout * 128:(dout + 1) * 128,
                                        c * TC:(c + 1) * TC]).then_inc(sxr.h, 16)
                sync.wait_ge(sob.h, gd + 1)
                sync.dma_start(out=x_out[dout * 128:(dout + 1) * 128,
                                         c * TC:(c + 1) * TC],
                               in_=ob[gd % 2][:]).then_inc(sst.h, 16)
        if lvl == 6:
            sync.wait_ge(sost.h, 16 * WNB)
        else:
            sync.wait_ge(sst.h, 16 * ND * NT)

    def f_gpsimd(g):
        g.dma_start(out=winmask[:], in_=winmask_d).then_inc(srt.h, 16)
        g.dma_start(out=sinks_k[:], in_=sinksk_d).then_inc(srt.h, 16)
        g.dma_start(out=sinks_v[:], in_=sinksv_d).then_inc(srt.h, 16)
        g.dma_start(out=ssel[:], in_=ssel_d).then_inc(srt.h, 16)
        if lvl == 0:
            return
        for c in range(NT):
            if lvl >= 3 and not norp:
                if c >= 1:
                    g.wait_ge(srp.h, rp[(c - 1) * 48 + 31][1])
                g.dma_start(out=rq[:], in_=ropes_d[:, c * 4 * TC:(c + 1) * 4 * TC]
                            ).then_inc(srl.h, 16)
            for gi in range(48):
                i = c * 48 + gi
                if i >= 2:
                    g.wait_ge(smb.h, i - 1)
                g.dma_start(out=wblk[i % 2][:], in_=wqkv[gi]).then_inc(swp.h, 16)
        if lvl <= 4:
            return
        g.wait_ge(scst.h, SCTOT)
        nc.gpsimd.collective_compute(
            "AllGather", ALU.bypass,
            replica_groups=[[0, 1], [2, 3], [4, 5], [6, 7]],
            ins=[cc_in.opt()], outs=[cc_out.opt()]).then_inc(scc.h, 1)
        g.wait_ge(scc.h, 1)
        g.dma_start(out=gbuf[:, 0:8192], in_=cc_out[0:128, :]).then_inc(sgl.h, 16)
        g.dma_start(out=gbuf[:, 8192:16384], in_=cc_out[128:256, :]).then_inc(sgl.h, 16)
        if cut == "B":
            return
        g.wait_ge(sqst.h, 16 * 128)
        g.wait_ge(skst.h, 16 * 128)
        g.wait_ge(svst.h, 16 * 256)
        g.wait_ge(sslst.h, 16 * 64)
        for b in range(WNB):
            if b >= 2:
                g.wait_ge(sma.h, 32 * (b - 1))
            g.dma_start(out=ke[b % 2][:], in_=ke_d[b]).then_inc(skl.h, 16)
            g.dma_start(out=ve[b % 2][:], in_=ve_d[b]).then_inc(skl.h, 16)
            for t_i in range(16):
                bt = b * 16 + t_i
                if bt >= 2:
                    g.wait_ge(sms.h, 2 * bt - 2)
                g.dma_start(out=qh[bt % 2][:],
                            in_=q_d[b][:, t_i * 128:(t_i + 1) * 128]).then_inc(sql.h, 16)
        if lvl == 6:
            return
        for c in range(NT):
            for dout in range(ND):
                gd = c * ND + dout
                if gd >= 2:
                    g.wait_ge(smo.h, gd - 1)
                g.dma_start(out=woblk[gd % 2][:], in_=wo[dout]).then_inc(swo.h, 16)

    def f_tensor(t):
        t.wait_ge(com.init.h, Common.READY)
        for c in range(NT):
            t.wait_ge(sx.h, 256 * (c + 1))
            ln.tensor_chunk(t, c, x_sb, xcol, PS[0], PS[1])
        if lvl == 0:
            return
        for c in range(NT):
            t.wait_ge(sxnl.h, 16 * (c + 1))
            for gi in range(48):
                i = c * 48 + gi
                t.wait_ge(swp.h, 16 * (i + 1))
                if lvl >= 2 and i >= 2:
                    t.wait_ge(sev.h, evc[i - 2])
                pg = PS[2 + i % 2]
                for k in range(ND):
                    mm = nc.tensor.matmul(pg[:], wblk[i % 2][:, k * 128:(k + 1) * 128],
                                          xnc[c % 2][:, xcol(k)],
                                          start=(k == 0), stop=(k == ND - 1))
                mm.then_inc(smb.h, 1)
                if gi >= 32 and lvl >= 3 and not novt:
                    t.wait_ge(sev.h, evc[i])
                    pv = prev_v(i)
                    if pv is not None:
                        t.wait_ge(srp.h, rp[pv][0] + 4)
                    for j in range(4):
                        nc.tensor.transpose(PSB[:, j * 128:(j + 1) * 128],
                                            vbf[i % 2][:, j * 128:(j + 1) * 128],
                                            com.ident_b[:]).then_inc(stpv.h, 1)
        if lvl <= 5:
            return
        t.wait_ge(sev.h, EVTOT)
        for b in range(WNB):
            t.wait_ge(skl.h, 32 * (b + 1))
            for t_i in range(16):
                bt = b * 16 + t_i
                t.wait_ge(sql.h, 16 * (bt + 1))
                for hoff in range(2):
                    it = bt * 2 + hoff
                    r0 = hoff * 64
                    if it >= 2:
                        t.wait_ge(sso.h, NSO * (it - 2) + 1)
                    pg = PS[2 + it % 2]
                    nc.tensor.matmul(pg[:, 0:384],
                                     qh[bt % 2][r0:r0 + 64, :],
                                     ke[b % 2][r0:r0 + 64, t_i * 128:t_i * 128 + 384],
                                     start=True, stop=True)
                    nc.tensor.matmul(pg[:, 384:400],
                                     qh[bt % 2][r0:r0 + 64, :],
                                     sinks_k[r0:r0 + 64, b * 16:(b + 1) * 16],
                                     start=True, stop=True).then_inc(sms.h, 1)
                    t.wait_ge(sso.h, NSO * it + 6)
                    t.wait_ge(stv.h, 4 * it)
                    for jj in range(3):
                        nc.tensor.transpose(PSB[:, 512 + jj * 128:512 + (jj + 1) * 128],
                                            pn[:, jj * 128:(jj + 1) * 128],
                                            com.ident_b[:]).then_inc(stp.h, 1)
                    nc.tensor.transpose(PSB[0:16, 896:1024], pn[:, 384:400],
                                        com.ident_b[:]).then_inc(stp.h, 1)
                    t.wait_ge(stv.h, 4 * (it + 1))
                    t.wait_ge(sov.h, it)
                    for jj in range(3):
                        nc.tensor.matmul(PS[6][0:64, 0:128],
                                         ve[b % 2][:, (t_i + jj) * 128 + r0:
                                                   (t_i + jj) * 128 + r0 + 64],
                                         ptb[:, jj * 128:(jj + 1) * 128],
                                         start=(jj == 0), stop=False)
                    nc.tensor.matmul(PS[6][0:64, 0:128],
                                     sinks_v[:, b * 128 + r0:b * 128 + r0 + 64],
                                     pts[:], start=False, stop=True).then_inc(sma.h, 1)
        if lvl == 6:
            return
        for c in range(NT):
            t.wait_ge(sol.h, 256 * (c + 1))
            for dout in range(ND):
                gd = c * ND + dout
                t.wait_ge(swo.h, 16 * (gd + 1))
                if gd >= 2:
                    t.wait_ge(sob.h, gd - 1)
                pd = PS[2 + gd % 2]
                for k in range(ND):
                    mm = nc.tensor.matmul(pd[:], woblk[gd % 2][:, k * 128:(k + 1) * 128],
                                          o_img[:, k * 512:(k + 1) * 512],
                                          start=(k == 0), stop=(k == ND - 1))
                mm.then_inc(smo.h, 1)

    def f_scalar(s):
        s.wait_ge(com.init.h, Common.READY)
        for c in range(NT):
            s.wait_ge(sx.h, 256 * (c + 1))
            ln.scalar_chunk(s, c, x_sb, xcol)
        if lvl == 0:
            return
        s.wait_ge(srt2.h, 32)
        for c in range(NT):
            s.wait_ge(sxnst.h, 256 * (c + 1))
            if c >= 2:
                s.wait_ge(smb.h, 48 * (c - 1))
            s.dma_start(out=xnc[c % 2][:], in_=xn_d[c]).then_inc(sxnl.h, 16)
            if lvl == 1:
                continue
            for gi in range(48):
                i = c * 48 + gi
                s.wait_ge(smb.h, i + 1)
                pg = PS[2 + i % 2]
                if gi < 32:
                    pq = prev_qk(i)
                    if lvl >= 3 and pq is not None and not norp:
                        s.wait_ge(srp.h, rp[pq][1])
                    nc.scalar.activation(qf[0:64, 0:TC], pg[0:64, :], AF.Identity,
                                         bias=biases[0:64, gi:gi + 1]).then_inc(sev.h, 1)
                    nc.scalar.activation(qf[0:64, TC:2 * TC], pg[64:128, :], AF.Identity,
                                         bias=biases[64:128, gi:gi + 1]).then_inc(sev.h, 1)
                else:
                    vg = vgl(i)
                    if lvl >= 3 and vg >= 1 and not novt:
                        s.wait_ge(stpv.h, 4 * vg)
                    nc.scalar.activation(vbf[i % 2][:], pg[:], AF.Identity,
                                         bias=biases[:, gi:gi + 1]).then_inc(sev.h, 1)
        if lvl <= 5:
            return
        for b in range(WNB):
            if b >= 1:
                s.wait_ge(sost.h, 16 * b)
            for t_i in range(16):
                for hoff in range(2):
                    it = (b * 16 + t_i) * 2 + hoff
                    s.wait_ge(sso.h, NSO * it + 2)
                    nc.scalar.activation(pb[:], ss[:], AF.Exp,
                                         bias=mxc[:, 0:1]).then_inc(sso.h, 1)
                    s.wait_ge(sma.h, it + 1)
                    nc.scalar.copy(o_blk[hoff * 64:(hoff + 1) * 64,
                                         t_i * 128:(t_i + 1) * 128],
                                   PS[6][0:64, 0:128]).then_inc(sov.h, 1)
            s.wait_ge(sov.h, 32 * (b + 1))
            s.dma_start(out=o_d[b], in_=o_blk[:]).then_inc(sost.h, 16)

    def f_vector(v):
        v.wait_ge(com.init.h, Common.READY)
        v.wait_ge(srt.h, 64)
        for c in range(NT):
            ln.vector_chunk(v, c, PS[0], PS[1])
            v.wait_ge(ln.s_bc.h, c + 1)
            if c >= 1:
                v.wait_ge(sxnst.h, 256 * c)
            emit_normalize(nc, v, snrm, snrm2, c * ND, x_sb, xcol, xn_sb,
                           xcol, PS[0], PS[1], sub3=sub3)
        if lvl <= 2:
            return
        for c in range(NT):
            if not norp:
                v.wait_ge(srl.h, 16 * (c + 1))
            for gi in range(32 if not norp else 0):
                i = c * 48 + gi
                bse = rp[i][0]
                isq = gi < 16
                cs = rq[:, 2 * TC:3 * TC] if isq else rq[:, 0:TC]
                sn = rq[:, 3 * TC:4 * TC] if isq else rq[:, TC:2 * TC]
                dsts = qst if isq else kst
                stsem = sqst if isq else skst
                ord_ = c * 16 + (gi if isq else gi - 16)
                v.wait_ge(sev.h, evc[i])
                if lvl >= 4 and ord_ >= 2:
                    v.wait_ge(stsem.h, 32 * (ord_ - 1))
                    if (not isq) and (ord_ - 2) in kcc:
                        v.wait_ge(scst.h, kcc[ord_ - 2])
                for hoff in range(2):
                    dA = dsts[2 * (i % 2) + hoff]
                    hc = slice(hoff * TC, (hoff + 1) * TC)
                    st = bse + 5 * hoff
                    nc.vector.tensor_tensor(rx[0:64, :], qf[0:64, hc], cs,
                                            ALU.mult).then_inc(srp.h, 1)
                    nc.vector.tensor_copy(ry[0:32, :], qf[32:64, hc]).then_inc(srp.h, 1)
                    nc.vector.tensor_copy(ry[32:64, :], qf[0:32, hc]).then_inc(srp.h, 1)
                    v.wait_ge(srp.h, st + 3)
                    nc.vector.tensor_tensor(ry[0:64, :], ry[0:64, :], sn,
                                            ALU.mult).then_inc(srp.h, 1)
                    v.wait_ge(srp.h, st + 4)
                    nc.vector.tensor_tensor(dA[0:64, :], rx[0:64, :], ry[0:64, :],
                                            ALU.add).then_inc(srp.h, 1)
            for gi in range(32, 48 if not novt else 32):
                i = c * 48 + gi
                vg = vgl(i)
                for j in range(4):
                    g_t = 4 * vg + j
                    v.wait_ge(stpv.h, 4 * (vg + 1))
                    if lvl >= 4 and g_t >= 4:
                        v.wait_ge(svst.h, 16 * max(0, g_t - 3))
                        if (g_t - 4) in vcc:
                            v.wait_ge(scst.h, vcc[g_t - 4])
                    nc.vector.tensor_copy(vtile[g_t % 4][:],
                                          PSB[:, j * 128:(j + 1) * 128]
                                          ).then_inc(srp.h, 1)
        if lvl <= 4:
            return
        v.wait_ge(sgl.h, 32)
        nc.vector.tensor_scalar_mul(slv[:, 0:2048], gbuf[:, 2048:4096],
                                    ssel[:, 0:1]).then_inc(ssl.h, 1)
        nc.vector.tensor_scalar_mul(slv[:, 2048:4096], gbuf[:, 8192:10240],
                                    ssel[:, 1:2]).then_inc(ssl.h, 1)
        v.wait_ge(sslst.h, 16 * 32)
        nc.vector.tensor_scalar_mul(slv[:, 0:2048], gbuf[:, 6144:8192],
                                    ssel[:, 2:3]).then_inc(ssl.h, 1)
        nc.vector.tensor_scalar_mul(slv[:, 2048:4096], gbuf[:, 12288:14336],
                                    ssel[:, 3:4]).then_inc(ssl.h, 1)
        if lvl == 5:
            return
        for b in range(WNB):
            for t_i in range(16):
                for hoff in range(2):
                    it = (b * 16 + t_i) * 2 + hoff
                    if t_i == 0:
                        msk = winmask[:, 400:800]
                    elif t_i == 15:
                        msk = winmask[:, 800:1200]
                    else:
                        msk = winmask[:, 0:400]
                    v.wait_ge(sms.h, it + 1)
                    if it >= 1:
                        v.wait_ge(sso.h, NSO * (it - 1) + 3)
                    nc.vector.tensor_tensor(ss[:], PS[2 + it % 2][:, 0:400], msk,
                                            ALU.add).then_inc(sso.h, 1)
                    v.wait_ge(sso.h, NSO * it + 1)
                    nc.vector.reduce_max(mxc[:, 0:1], ss[:], axis=AX.X,
                                         negate=True).then_inc(sso.h, 1)
                    v.wait_ge(sso.h, NSO * it + 3)
                    nc.vector.reduce_sum(zzc[:, 0:1], pb[:], axis=AX.X
                                         ).then_inc(sso.h, 1)
                    v.wait_ge(sso.h, NSO * it + 4)
                    nc.vector.reciprocal(zzc[:, 1:2], zzc[:, 0:1]).then_inc(sso.h, 1)
                    v.wait_ge(sso.h, NSO * it + 5)
                    if it >= 1:
                        v.wait_ge(stp.h, 4 * it)
                    nc.vector.tensor_scalar_mul(pn[:], pb[:], zzc[:, 1:2]
                                                ).then_inc(sso.h, 1)
                    for jj in range(4):
                        v.wait_ge(stp.h, 4 * (it + 1))
                        if jj == 0:
                            v.wait_ge(sma.h, it)
                        if jj < 3:
                            nc.vector.tensor_copy(ptb[:, jj * 128:(jj + 1) * 128],
                                                  PSB[:, 512 + jj * 128:512 + (jj + 1) * 128]
                                                  ).then_inc(stv.h, 1)
                        else:
                            nc.vector.tensor_copy(pts[:], PSB[0:16, 896:1024]
                                                  ).then_inc(stv.h, 1)
        if lvl == 6:
            return
        for c in range(NT):
            for dout in range(ND):
                gd = c * ND + dout
                v.wait_ge(smo.h, gd + 1)
                v.wait_ge(sxr.h, 16 * (gd + 1))
                if gd >= 2:
                    v.wait_ge(sst.h, 16 * (gd - 1))
                nc.vector.scalar_tensor_tensor(ob[gd % 2][:], PS[2 + gd % 2][:],
                                               bo_col[:, dout:dout + 1],
                                               xr[gd % 2][:], ALU.add, ALU.add
                                               ).then_inc(sob.h, 1)

    dn = {0: (sxnst, 16 * 64), 1: (smb, 192), 2: (sev, EVTOT),
          3: (srp, rp[NT * 48 - 1][1]), 4: (svst, 16 * 256),
          5: (sslst, 16 * 64), 6: (sost, 16 * WNB),
          7: (sst, 16 * ND * NT)}[lvl]
    return dict(sync=f_sync, gpsimd=f_gpsimd, tensor=f_tensor,
                scalar=f_scalar, vector=f_vector, done=dn)




def assemble(p: Prog, com: Common, emitters):
    with p.nc.Block() as block:
        @block.vector
        def _(v):
            com.emit_vector(v)
            for e in emitters:
                e["vector"](v)

        @block.gpsimd
        def _(g):
            com.emit_gpsimd(g)
            for e in emitters:
                e["gpsimd"](g)

        @block.sync
        def _(s):
            for e in emitters:
                e["sync"](s)

        @block.tensor
        def _(t):
            for e in emitters:
                e["tensor"](t)

        @block.scalar
        def _(s):
            for e in emitters:
                e["scalar"](s)
    p.ctx.__exit__(None, None, None)
    return p.nc




def prep_lhsT(W):
    O, Kd = W.shape
    return np.ascontiguousarray(
        W.reshape(O // 128, 128, Kd // 128, 128).transpose(0, 3, 2, 1)
    ).reshape(O // 128, 128, Kd)


def make_carrysel():
    sels = []
    for c in range(NCORES):
        s = np.zeros((128, 2), np.float32)
        if c % 2 == 1:
            s[:, 0] = 1.0
        sels.append(s)
    return sels


def prep_mla_weights(Wq, Wkva, Wkvb, Wo, kvnorm, cos, sin):
    Wq_r = np.zeros((16, 256, 2048), np.float32)
    for h in range(16):
        Wq_r[h, 0:128] = Wq[h * 192:h * 192 + 128]
        Wq_r[h, 128:192] = Wq[h * 192 + 128:(h + 1) * 192]
    wq_blk = prep_lhsT(Wq_r.reshape(4096, 2048))
    Wkva_p = np.zeros((640, 2048), np.float32)
    Wkva_p[0:576] = Wkva
    wkva_blk = prep_lhsT(Wkva_p)
    wkvb_k = np.stack([prep_lhsT(Wkvb[h * 256:h * 256 + 128, :])[0]
                       for h in range(16)])
    wkvb_v = np.stack([
        np.ascontiguousarray(
            Wkvb[h * 256 + 128:(h + 1) * 256, :].T.reshape(4, 128, 128)
            .transpose(1, 0, 2).reshape(128, 512))
        for h in range(16)])
    wo_blk = prep_lhsT(Wo)
    gamma = np.ascontiguousarray(kvnorm.reshape(4, 128).T)
    cossin = np.concatenate([np.asarray(cos).reshape(64),
                             np.asarray(sin).reshape(64)])[None, :].astype(np.float32)
    return wq_blk, wkva_blk, wkvb_k, wkvb_v, wo_blk, gamma, cossin


def prep_gqa_weights(Wq, Wk, Wv, Wo, core_id):
    wq_blk = prep_lhsT(np.ascontiguousarray(Wq.T))
    wk_blk = prep_lhsT(np.ascontiguousarray(Wk.T))
    wv_img = np.ascontiguousarray(
        Wv.reshape(16, 128, 512).transpose(1, 0, 2).reshape(128, 16 * 512))
    wo_blk = prep_lhsT(np.ascontiguousarray(Wo.T))
    return wq_blk, wk_blk, wv_img, wo_blk


def gqa_tables(core_id):
    half = core_id % 2
    pos = np.arange(half * TOK, (half + 1) * TOK, dtype=np.float64)
    dh = 128
    inv = 1.0 / (10000.0 ** (np.arange(0, dh, 2, dtype=np.float64) / dh))
    ang = pos[None, :] * np.concatenate([inv, inv])[:, None]
    cos = np.cos(ang)
    sin = np.sin(ang)
    sin_eff = sin.copy()
    sin_eff[0:64] = -sin[0:64]
    sc = 128.0 ** -0.5
    ropes = np.concatenate([cos, sin_eff, cos * sc, sin_eff * sc],
                           axis=0).astype(np.float32)
    maskv = np.zeros((128, NT * 32), np.float32)
    for g in range(NT):
        start = half * TOK + g * TC
        for kc in range(32):
            kpos = kc * 128 + np.arange(128)
            maskv[:, g * 32 + kc] = np.where(kpos < start, 0.0, -1e30)
    tri = np.zeros((4, 128, TC), np.float32)
    for i in range(4):
        colv = np.arange(TC)[None, :] <= (i * 128 + np.arange(128))[:, None]
        tri[i] = np.where(colv, 0.0, -1e30)
    trimask = np.ascontiguousarray(tri.transpose(1, 0, 2).reshape(128, 4 * TC))
    return ropes, maskv, trimask


def prep_genes_weights(Wc, genes, Wd, Wu, norm_w, temperature, scale):
    wc_blk = prep_lhsT(Wc)
    genes_blk = np.ascontiguousarray(genes.reshape(NG, 128, 256))
    wd_img = np.ascontiguousarray(
        prep_lhsT(Wd).transpose(1, 0, 2).reshape(128, ND * 256))
    wu_img = np.ascontiguousarray(
        prep_lhsT(Wu).transpose(1, 0, 2).reshape(128, ND * 256))
    wdwu = np.stack([wd_img, wu_img])
    normw_h = np.ascontiguousarray(norm_w.reshape(ND, 128).T)
    tscal = np.array([[temperature, scale]], np.float32)
    return wc_blk, genes_blk, wdwu, normw_h, tscal


def prep_win_weights(Wqkv, bqkv, Wo, bo, sinks):
    wqkv_blk = prep_lhsT(np.asarray(Wqkv))
    wo_blk = prep_lhsT(np.asarray(Wo))
    bias = np.zeros((128, 64), np.float32)
    bq = np.asarray(bqkv).reshape(48, 128)
    for j in range(48):
        bias[:, j] = bq[j]
    bb = np.asarray(bo).reshape(16, 128)
    for j in range(16):
        bias[:, 48 + j] = bb[j]
    sk = np.zeros((128, 256), np.float32)
    s = np.asarray(sinks)
    for b in range(16):
        for hoff in range(2):
            sk[hoff * 64:(hoff + 1) * 64, b * 16:(b + 1) * 16] = s[2 * b + hoff].T
    sv = np.ascontiguousarray(s.transpose(1, 0, 2).reshape(16, 2048)).astype(np.float32)
    return wqkv_blk, wo_blk, bias, sk, sv


def win_tables(core_id):
    half = core_id % 2
    pos = np.arange(half * TOK, (half + 1) * TOK, dtype=np.float64)
    inv = 1.0 / 10000.0 ** (np.arange(32, dtype=np.float64) / 32.0)
    ang = inv[:, None] * pos[None, :]
    cos, sin = np.cos(ang), np.sin(ang)
    cos64 = np.concatenate([cos, cos], axis=0)
    sneff = np.concatenate([-sin, sin], axis=0)
    sc = 0.125
    ropes = np.zeros((64, NT * 4 * TC), np.float32)
    for c in range(NT):
        tsl = slice(c * TC, (c + 1) * TC)
        ropes[:, c * 4 * TC + 0 * TC:c * 4 * TC + 1 * TC] = cos64[:, tsl]
        ropes[:, c * 4 * TC + 1 * TC:c * 4 * TC + 2 * TC] = sneff[:, tsl]
        ropes[:, c * 4 * TC + 2 * TC:c * 4 * TC + 3 * TC] = cos64[:, tsl] * sc
        ropes[:, c * 4 * TC + 3 * TC:c * 4 * TC + 4 * TC] = sneff[:, tsl] * sc
    NEG = -1e30
    p_idx = np.arange(128)[:, None]
    j_idx = np.arange(384)[None, :]
    band = np.where((j_idx >= p_idx + 64) & (j_idx <= p_idx + 192), 0.0, NEG)
    base = np.concatenate([band, np.zeros((128, 16))], axis=1)
    t0 = base.copy()
    if half == 0:
        t0[:, 0:128] = NEG
    t15 = base.copy()
    if half == 1:
        t15[:, 256:384] = NEG
    winmask = np.concatenate([base, t0, t15], axis=1).astype(np.float32)
    ssel = np.zeros((128, 4), np.float32)
    ssel[:, 0] = 1.0 if half == 1 else 0.0
    ssel[:, 1] = 1.0 if half == 0 else 0.0
    ssel[:, 2] = 1.0 if half == 1 else 0.0
    ssel[:, 3] = 1.0 if half == 0 else 0.0
    return ropes, winmask, ssel


def prep_ffn_weights(w_gate_up, w_down):
    wgu = np.ascontiguousarray(
        w_gate_up.reshape(128, 128, 16, 128).transpose(0, 3, 2, 1)
    ).reshape(128, 128, 2048)
    wd = np.ascontiguousarray(
        w_down.reshape(16, 128, 64, 128).transpose(0, 3, 2, 1)
    ).reshape(16, 128, 8192)
    return wgu, wd


def shard_x(x):
    out = []
    for c in range(NCORES):
        b, h = c // 2, c % 2
        out.append(np.ascontiguousarray(x[b, h * TOK:(h + 1) * TOK, :].T))
    return out


def unshard_x(xs):
    x = np.empty((4, SEQ, D), np.float32)
    for c in range(NCORES):
        b, h = c // 2, c % 2
        x[b, h * TOK:(h + 1) * TOK, :] = xs[c].T
    return x


def kernel(**inputs):
    z = {k: np.asarray(v) for k, v in inputs.items()}

    aa_wqkv = prep_lhsT(np.asarray(z["L0_attn_a_W_qkv"], np.float32))
    aa_wout = prep_lhsT(np.asarray(z["L0_attn_a_W_out"], np.float32))
    aa_sels = make_carrysel()
    f_wgu, f_wd = prep_ffn_weights(np.asarray(z["L0_ffn_w_gate_up"], np.float32),
                                   np.asarray(z["L0_ffn_w_down"], np.float32))
    m_wq, m_wkva, m_wkbk, m_wkbv, m_wo, m_gamma, m_cossin = prep_mla_weights(
        np.asarray(z["L0_attn_b_Wq"], np.float32),
        np.asarray(z["L0_attn_b_Wkva"], np.float32),
        np.asarray(z["L0_attn_b_Wkvb"], np.float32),
        np.asarray(z["L0_attn_b_Wo"], np.float32),
        np.asarray(z["L0_attn_b_kv_a_norm_weight"], np.float32),
        np.asarray(z["L0_attn_b_cos"], np.float32),
        np.asarray(z["L0_attn_b_sin"], np.float32))
    g_wq, g_wk, g_wv, g_wo = prep_gqa_weights(
        np.asarray(z["L1_attn_a_Wq"], np.float32),
        np.asarray(z["L1_attn_a_Wk"], np.float32),
        np.asarray(z["L1_attn_a_Wv"], np.float32),
        np.asarray(z["L1_attn_a_Wo"], np.float32), 0)
    gq_tabs = [gqa_tables(i) for i in range(NCORES)]
    e_wc, e_genes, e_wdwu, e_normw, e_tscal = prep_genes_weights(
        np.asarray(z["L1_ffn_Wc"], np.float32),
        np.asarray(z["L1_ffn_genes"], np.float32),
        np.asarray(z["L1_ffn_Wd"], np.float32),
        np.asarray(z["L1_ffn_Wu"], np.float32),
        np.asarray(z["L1_ffn_norm_w"], np.float32),
        float(np.asarray(z["L1_ffn_temperature"]).reshape(-1)[0]),
        float(np.asarray(z["L1_ffn_scale"]).reshape(-1)[0]))
    w_wqkv, w_wo, w_bias, w_sk, w_sv = prep_win_weights(
        np.asarray(z["L1_attn_b_Wqkv"], np.float32),
        np.asarray(z["L1_attn_b_bqkv"], np.float32),
        np.asarray(z["L1_attn_b_Wo"], np.float32),
        np.asarray(z["L1_attn_b_bo"], np.float32),
        np.asarray(z["L1_attn_b_sinks"], np.float32))
    w_tabs = [win_tables(i) for i in range(NCORES)]

    def din(p, name, arr):
        return p.dram(name, list(arr.shape), F32, kind="ExternalInput")

    def b_attn_a(p, com, xi, xo):
        wq = din(p, "wqkv", aa_wqkv)
        wo = din(p, "wout", aa_wout)
        cs = p.dram("csel", [128, 2], F32, kind="ExternalInput")
        em = emit_attn_a(p, com, "a0", xi, xo, wq, wo, cs)
        return em, lambda i: {"wqkv": aa_wqkv, "wout": aa_wout,
                              "csel": aa_sels[i]}

    def b_ffn(p, com, xi, xo):
        wgu = din(p, "wgu", f_wgu)
        wd = din(p, "wd", f_wd)
        em = emit_ffn(p, com, "f0", xi, xo, wgu, wd)
        return em, lambda i: {"wgu": f_wgu, "wd": f_wd}

    def b_mla(p, com, xi, xo):
        wq = din(p, "wq", m_wq)
        wkva = din(p, "wkva", m_wkva)
        wkbk = din(p, "wkbk", m_wkbk)
        wkbv = din(p, "wkbv", m_wkbv)
        wo = din(p, "wo", m_wo)
        gam = p.dram("gam", [128, 4], F32, kind="ExternalInput")
        cs = p.dram("cs", [1, 128], F32, kind="ExternalInput")
        maskv = p.dram("maskv", [128, NT * 32], F32, kind="ExternalInput")
        trid = p.dram("trid", [128, 4 * TC], F32, kind="ExternalInput")
        em = emit_attn_mla(p, com, "ma", xi, xo, wq, wkva, wkbk, wkbv, wo,
                           gam, cs, maskv, trid)
        return em, lambda i: {"wq": m_wq, "wkva": m_wkva, "wkbk": m_wkbk,
                              "wkbv": m_wkbv, "wo": m_wo, "gam": m_gamma,
                              "cs": m_cossin, "maskv": gq_tabs[i][1],
                              "trid": gq_tabs[i][2]}

    def b_gqa(p, com, xi, xo):
        wq = din(p, "wq", g_wq)
        wk = din(p, "wk", g_wk)
        wv = din(p, "wv", g_wv)
        wo = din(p, "wo", g_wo)
        ropes = p.dram("ropes", [512, TOK], F32, kind="ExternalInput")
        maskv = p.dram("maskv", [128, NT * 32], F32, kind="ExternalInput")
        trid = p.dram("trid", [128, 4 * TC], F32, kind="ExternalInput")
        em = emit_attn_gqa(p, com, "ga", xi, xo, wq, wk, wv, wo, ropes,
                           maskv, trid)
        return em, lambda i: {"wq": g_wq, "wk": g_wk, "wv": g_wv, "wo": g_wo,
                              "ropes": gq_tabs[i][0], "maskv": gq_tabs[i][1],
                              "trid": gq_tabs[i][2]}

    def b_genes(p, com, xi, xo):
        wc = din(p, "wc", e_wc)
        genes_d = din(p, "genes", e_genes)
        wdwu = din(p, "wdwu", e_wdwu)
        normw = p.dram("normw", [128, ND], F32, kind="ExternalInput")
        tscal = p.dram("tscal", [1, 2], F32, kind="ExternalInput")
        em = emit_ffn_genes(p, com, "g0", xi, xo, wc, genes_d, wdwu,
                            normw, tscal)
        return em, lambda i: {"wc": e_wc, "genes": e_genes, "wdwu": e_wdwu,
                              "normw": e_normw, "tscal": e_tscal}

    def b_win(p, com, xi, xo):
        wq = din(p, "wqkv", w_wqkv)
        wo = din(p, "wo", w_wo)
        bias = p.dram("bias", [128, 64], F32, kind="ExternalInput")
        ropes = p.dram("ropes", [64, NT * 4 * TC], F32, kind="ExternalInput")
        wmask = p.dram("winmask", [128, 1200], F32, kind="ExternalInput")
        sk = p.dram("sinksk", [128, 256], F32, kind="ExternalInput")
        sv = p.dram("sinksv", [16, 2048], F32, kind="ExternalInput")
        ssel = p.dram("ssel", [128, 4], F32, kind="ExternalInput")
        em = emit_attn_win(p, com, "wa", xi, xo, wq, wo, bias, ropes,
                           wmask, sk, sv, ssel)
        return em, lambda i: {"wqkv": w_wqkv, "wo": w_wo, "bias": w_bias,
                              "ropes": w_tabs[i][0], "winmask": w_tabs[i][1],
                              "sinksk": w_sk, "sinksv": w_sv,
                              "ssel": w_tabs[i][2]}

    def launch(build_fn, xs):
        p = Prog(num_devices=8, detect_races=False)
        com = Common(p)
        xi = p.dram("x_in", [D, TOK], F32, kind="ExternalInput")
        xo = p.dram("x_out", [D, TOK], F32, kind="ExternalOutput")
        em, mk = build_fn(p, com, xi, xo)
        nc = assemble(p, com, [em])
        in_maps = [dict(x_in=np.ascontiguousarray(xs[i], dtype=np.float32),
                        **mk(i)) for i in range(NCORES)]
        res = run_bass_kernel_spmd(nc, in_maps, core_ids=list(range(NCORES)))
        return [res.results[i]["x_out"] for i in range(NCORES)]

    xs = shard_x(np.asarray(z["L0_attn_a_x"], np.float32))
    for stage in (b_attn_a, b_ffn, b_mla, b_ffn, b_gqa, b_genes, b_win,
                  b_genes):
        xs = launch(stage, xs)
    return unshard_x(xs)
